# version 09
# baseline (speedup 1.0000x reference)
import os
os.environ.setdefault("NEURON_SCRATCHPAD_PAGE_SIZE", "512")

import numpy as np
import ml_dtypes
from contextlib import ExitStack

import concourse.bass as bass
import concourse.mybir as mybir
from concourse.bass_utils import run_bass_kernel_spmd

F32 = mybir.dt.float32
BF16 = mybir.dt.bfloat16
AF = mybir.ActivationFunctionType
ALU = mybir.AluOpType
AX = mybir.AxisListType

B, L, D, H = 2, 4096, 4096, 32
DFF, NFB = 11008, 86
T = 1024
PADS = [4, 8, 12, 16, 20, 24, 28, 32]
NEG = -1.0e9
SCALE = float(1.0 / np.sqrt(128.0))
VST = 144

SZ_WG = NFB * 32 * 128 * 128
SZ_WD = 8 * NFB * 128 * 512
SZ_WO = 8 * 32 * 128 * 512
E_W = 2 * SZ_WG + SZ_WD + SZ_WO
OFF_WU, OFF_WD, OFF_WO = SZ_WG, 2 * SZ_WG, 2 * SZ_WG + SZ_WD

DEBUG_OUTS = False


def rank_blocks(r):
    return [r, 7 - r, 8 + r, 15 - r, 16 + r, 23 - r, 24 + r, 31 - r]


def block_owner(j):
    m, u = divmod(j, 8)
    return (u, 2 * m) if u < 4 else (7 - u, 2 * m + 1)


def att_schedule():
    sched, off = [], 0
    for g in range(2):
        slots = [4 * g + i for i in range(4)]
        for j in range(PADS[slots[-1]]):
            act = [s for s in slots if PADS[s] > j]
            sched.append((g, j, act, off))
            off += 128 * len(act)
    return sched


SCHED = att_schedule()
MASK_COLS = 144 * 128


class Sem:
    def __init__(self, handle, name):
        self.h, self.name, self.count = handle, name, 0


class Rec:
    ENGINES = ("tensor", "vector", "scalar", "gpsimd", "sync")

    def __init__(self):
        self.q = {e: [] for e in self.ENGINES}

    def emit(self, eng, fn, waits=(), inc=None, inc_n=1):
        self.q[eng].append((list(waits), fn, inc, inc_n))
        if inc is not None:
            inc.count += inc_n

    def collect_targets(self):
        t = {}
        for q in self.q.values():
            for waits, _, _, _ in q:
                for s, v in waits:
                    t.setdefault(id(s), set()).add(v)
        return t

    def replay(self, name, engobj, targets):
        cum = {}
        pending = {}
        for waits, fn, inc, inc_n in self.q[name]:
            merged = {}
            for s, v in waits:
                if v > 0 and v > merged.get(id(s), (None, -1))[1]:
                    merged[id(s)] = (s, v)
            for s, v in merged.values():
                engobj.wait_ge(s.h, v)
            if fn is not None:
                inst = fn(engobj)
                if inc is not None:
                    if inc_n == 16:
                        inst.then_inc(inc.h, inc_n)
                        continue
                    c = cum.get(id(inc), 0) + inc_n
                    cum[id(inc)] = c
                    p = pending.get(id(inc), 0) + inc_n
                    if c in targets.get(id(inc), ()):
                        inst.then_inc(inc.h, p)
                        pending[id(inc)] = 0
                    else:
                        pending[id(inc)] = p


def build_program():
    nc = bass.Bass("TRN2", target_bir_lowering=False)
    dt = nc.dram_tensor

    x_in = dt("x_core", [T, D], F32, kind="ExternalInput").ap()
    wq_t = dt("wq_t", [H, 32, 128, 128], F32, kind="ExternalInput").ap()
    wk_t = dt("wk_t", [H, 32, 128, 128], F32, kind="ExternalInput").ap()
    wv_t = dt("wv_t", [8, 32, 128, 512], F32, kind="ExternalInput").ap()
    wslice = dt("wslice", [E_W // 8 // 512, 512], F32, kind="ExternalInput").ap()
    bq_in = dt("bq_t", [128, 32], F32, kind="ExternalInput").ap()
    bk_in = dt("bk_t", [128, 32], F32, kind="ExternalInput").ap()
    bv_in = dt("bv_row", [1, D], F32, kind="ExternalInput").ap()
    bo_in = dt("bo_row", [1, D], F32, kind="ExternalInput").ap()
    masks_in = dt("masks_t", [128, MASK_COLS], BF16, kind="ExternalInput").ap()
    ident_in = dt("ident", [128, 128], BF16, kind="ExternalInput").ap()
    out_dram = dt("out_core", [T, D], F32, kind="ExternalOutput").ap()
    if DEBUG_OUTS:
        dbg_xn = dt("dbg_xn", [128, 32 * T], BF16, kind="ExternalOutput").ap()
        dbg_kv = dt("dbg_kv", [4 * 16400, 512], BF16, kind="ExternalOutput").ap()
        dbg_ot = dt("dbg_ot", [D, T], BF16, kind="ExternalOutput").ap()

    KVROWS = 16400
    kv_src = dt("kv_src", [KVROWS, 512], BF16, kind="Internal")
    kv_gath = dt("kv_gath", [4 * KVROWS, 512], BF16, kind="Internal")
    ot_dram = dt("ot_dram", [D, T], BF16, kind="Internal")
    xn_spill = dt("xn_spill", [128, 32 * T], BF16, kind="Internal")

    KSZ = H * 128 * T
    kview = kv_src.ap().rearrange("a b -> (a b)")[0:KSZ] \
        .rearrange("(h p t) -> h p t", h=H, p=128)
    vview = kv_src.ap().rearrange("a b -> (a b)")[KSZ:2 * KSZ] \
        .rearrange("(t d) -> t d", t=T)
    norms_row = kv_src.ap()[16384:16385, 0:H]

    gsp = kv_gath.ap().rearrange("a b -> (a b)").rearrange("(r z) -> r z", r=4)
    gK = gsp[:, 0:KSZ].rearrange("r (h p s t) -> h p r s t", h=H, p=128, s=8)
    gV = gsp[:, KSZ:2 * KSZ].rearrange("r (s t h c) -> h t r s c", s=8, t=128, h=H)
    gN = gsp[:, 16384 * 512:16384 * 512 + H].rearrange("r h -> h r")

    ag_w_in = dt("ag_w_in", [E_W // 8 // 512, 512], BF16, kind="Internal")
    ag_w_out = dt("ag_w_out", [E_W // 512, 512], BF16, kind="Internal", addr_space="Shared")
    wff = ag_w_out.ap().rearrange("a b -> (a b)")
    wg_view = wff[0:SZ_WG].rearrange("(fb dc p f) -> fb p dc f", fb=NFB, dc=32, p=128)
    wu_view = wff[OFF_WU:OFF_WU + SZ_WG].rearrange("(fb dc p f) -> fb p dc f", fb=NFB, dc=32, p=128)
    wd_view = wff[OFF_WD:OFF_WD + SZ_WD].rearrange("(do fb p f) -> do p fb f", do=8, fb=NFB, p=128)
    wo_view = wff[OFF_WO:OFF_WO + SZ_WO].rearrange("(do ic p f) -> do p ic f", do=8, ic=32, p=128)

    st = ExitStack()
    e = st.enter_context

    def sem(name):
        return Sem(e(nc.semaphore(name)), name)

    s_x = sem("s_x")
    s_w = sem("s_w")
    s_kvw = sem("s_kvw")
    s_gr = sem("s_gr")
    s_ot = sem("s_ot")
    s_sp = sem("s_sp")
    s_ms = sem("s_ms")
    s_xr = sem("s_xr")
    s_out = sem("s_out")
    s_cc = sem("s_cc")
    s_cv = sem("s_cv")
    s_wu = sem("s_wu")
    tA, vA, aA, gA = sem("tA"), sem("vA"), sem("aA"), sem("gA")

    rec = Rec()

    def EM(eng, fn, waits=(), inc=None, inc_n=1):
        waits = list(waits)
        if fn is not None:
            if eng == "vector":
                waits.append((vA, vA.count))
            elif eng == "scalar":
                waits.append((aA, aA.count))
        rec.emit(eng, fn, waits, inc, inc_n)

    def DMA(eng, fn, waits, s):
        rec.emit(eng, fn, waits, inc=s, inc_n=16)

    sb_ident = e(nc.sbuf_tensor("sb_ident", [128, 128], BF16))
    sb_ones = e(nc.sbuf_tensor("sb_ones", [128, 128], BF16))
    sb_bq = e(nc.sbuf_tensor("sb_bq", [128, 32], F32))
    sb_bk = e(nc.sbuf_tensor("sb_bk", [128, 32], F32))
    sb_bvp = e(nc.sbuf_tensor("sb_bvp", [128, D], BF16))
    sb_bop = e(nc.sbuf_tensor("sb_bop", [128, D], BF16))
    sb_sm = e(nc.sbuf_tensor("sb_sm", [128, 16], F32))
    sb_bc = e(nc.sbuf_tensor("sb_bc", [128, 16], BF16))
    sb_nk = e(nc.sbuf_tensor("sb_nk", [1, 128], BF16))
    (C_S1, C_S2, C_MEAN, C_VAR, C_STD, C_RSTD, C_MRS, C_M2,
     C_MQ, C_MK, C_T0, C_T1, C_INVL, C_MHB, C_U1, C_U2) = range(16)

    def smc(i):
        return sb_sm[:, i:i + 1]

    base = (nc.sbuf_base + 63) & ~63
    top = nc.sbuf_top

    def at(name, shape, dtype, off):
        nbytes = shape[1] * (2 if dtype == BF16 else 4)
        assert off + nbytes <= top, f"{name}: {off}+{nbytes} > {top}"
        return nc.alloc_sbuf_tensor_at(name, shape, dtype, offset=off)

    xn_d = at("xn_d", [128, 32 * T], BF16, base)
    SUB = base + 65536
    xt = [at(f"xt{i}", [128, D], F32, SUB + i * 16384) for i in range(2)]
    xsq = at("xsq", [128, D], F32, SUB + 32768)
    xnt = at("xnt", [128, D], BF16, SUB + 49152)
    wkb = [at(f"wkb{i}", [128, 32 * 128], BF16, SUB + i * 8192) for i in range(2)]
    kqb = [at(f"kqb{i}", [128, T], BF16, SUB + 16384 + i * 2048) for i in range(2)]
    ksq = at("ksq", [128, T], BF16, SUB + 20480)
    nacc = at("nacc", [1, 32], BF16, SUB + 22528)
    wvb = [at(f"wvb{i}", [128, 32 * 512], BF16, SUB + 23040 + i * 32768) for i in range(2)]
    vtb = [at(f"vtb{i}", [128, 512], BF16, SUB + 88576 + i * 1024) for i in range(2)]
    sb_mask = at("sb_mask", [128, MASK_COLS], BF16, SUB)
    kh = [at(f"kh{i}", [128, 32 * 128], BF16, SUB + 36864 + i * 8192) for i in range(2)]
    vh = [at(f"vh{i}", [128, 32 * VST], BF16, SUB + 53248 + i * 9216) for i in range(2)]
    wqb = [at(f"wqb{i}", [128, 32 * 128], BF16, SUB + 71680 + i * 8192) for i in range(2)]
    qh = at("qh", [128, T], BF16, SUB + 90624)
    qsq = at("qsq", [128, T], BF16, SUB + 92672)
    ptb = [at(f"ptb{i}", [128, 512], BF16, SUB + 96768 + i * 1024) for i in range(4)]
    osb = at("osb", [128, 128], BF16, SUB + 100864)
    oth = [at(f"oth{i}", [128, T], BF16, SUB + 101120 + i * 2048) for i in range(2)]
    assert SUB + 105216 <= top, SUB + 105216

    h_sb = at("h_sb", [128, NFB * 512], BF16, base)
    xnh = at("xnh", [128, 32 * 512], BF16, base + 88064)
    otx = at("otx", [128, 32 * 512], BF16, base + 120832)
    PB = base + 153600
    wgb = [at(f"wgb{i}", [128, 32 * 128], BF16, PB + i * 8192) for i in range(2)]
    wub = [at(f"wub{i}", [128, 32 * 128], BF16, PB + 16384 + i * 8192) for i in range(2)]
    gsb = [at(f"gsb{i}", [128, 512], BF16, PB + 32768 + i * 1024) for i in range(2)]
    wdb = [at(f"wdb{i}", [128, 8 * 512], BF16, PB + i * 8192) for i in range(2)]
    wob = [at(f"wob{i}", [128, 4 * 512], BF16, PB + 16384 + i * 4096) for i in range(2)]
    xrb = [at(f"xrb{i}", [128, 512], F32, PB + 24576 + i * 2048) for i in range(2)]
    orb = [at(f"orb{i}", [128, 512], F32, PB + 28672 + i * 2048) for i in range(2)]
    assert PB + 34816 <= top

    pb_y = [e(nc.psum_tensor(f"pb_y{i}", [128, 512], F32)) for i in range(4)]
    pb_s = [e(nc.psum_tensor(f"pb_s{i}", [128, 512], F32)) for i in range(2)]
    pb_t = [e(nc.psum_tensor(f"pb_t{i}", [128, 256], BF16)) for i in range(2)]

    DMA("sync", lambda s: s.dma_start(out=sb_ident[:, :], in_=ident_in), [], s_ms)
    DMA("sync", lambda s: s.dma_start(out=sb_bq[:, :], in_=bq_in), [], s_ms)
    DMA("sync", lambda s: s.dma_start(out=sb_bk[:, :], in_=bk_in), [], s_ms)
    s_const = s_ms.count
    EM("gpsimd", lambda g: g.memset(sb_ones[:, :], 1.0), [], gA)
    EM("gpsimd", lambda g: g.memset(sb_bvp[:, :], 0.0), [], gA)
    EM("gpsimd", lambda g: g.memset(sb_bop[:, :], 0.0), [], gA)
    EM("gpsimd", lambda g: g.memset(sb_bc[:, :], 0.0), [], gA)
    g_const = gA.count
    DMA("gpsimd", lambda g: g.dma_start(out=sb_bvp[0:1, :], in_=bv_in), [(gA, g_const)], s_w)
    DMA("gpsimd", lambda g: g.dma_start(out=sb_bop[0:1, :], in_=bo_in), [(gA, g_const)], s_w)
    EM("vector", lambda v: v.tensor_scalar_mul(sb_bq[:, :], sb_bq[:, :], SCALE),
       [(s_ms, s_const)], vA)
    v_bq = vA.count

    a_lnapply = 0
    t_tr_done = {0: 0, 1: 0}
    v_tr_done = {0: 0, 1: 0}
    for tt in range(8):
        bp = tt % 2
        w = [(aA, a_lnapply)] if tt >= 2 else []
        DMA("sync", lambda s, tt=tt, bp=bp: s.dma_start(
            out=xt[bp][:, :], in_=x_in[tt * 128:(tt + 1) * 128, :]), w, s_x)
        wx = [(s_x, s_x.count)]
        EM("vector", lambda v, bp=bp: v.reduce_sum(smc(C_S1), xt[bp][:, :], axis=AX.X), wx, vA)
        EM("scalar", lambda s, bp=bp: s.activation(xsq[:, :], xt[bp][:, :], AF.Square),
           wx + [(vA, vA.count)], aA)
        EM("vector", lambda v: v.reduce_sum(smc(C_S2), xsq[:, :], axis=AX.X), [(aA, aA.count)], vA)
        EM("vector", lambda v: v.tensor_scalar_mul(smc(C_MEAN), smc(C_S1), 1.0 / D), [], vA)
        EM("vector", lambda v: v.tensor_mul(smc(C_M2), smc(C_MEAN), smc(C_MEAN)), [], vA)
        EM("vector", lambda v: v.tensor_scalar(smc(C_VAR), smc(C_S2), 1.0 / D, None, ALU.mult),
           [(aA, a_lnapply)], vA)
        EM("vector", lambda v: v.tensor_sub(smc(C_VAR), smc(C_VAR), smc(C_M2)), [], vA)
        EM("vector", lambda v: v.tensor_scalar_add(smc(C_VAR), smc(C_VAR), 1e-5), [], vA)
        EM("scalar", lambda s: s.activation(smc(C_STD), smc(C_VAR), AF.Sqrt), [(vA, vA.count)], aA)
        EM("vector", lambda v: v.reciprocal(smc(C_RSTD), smc(C_STD)),
           [(aA, aA.count)], vA)
        EM("vector", lambda v: v.tensor_mul(smc(C_MRS), smc(C_MEAN), smc(C_RSTD)), [], vA)
        EM("vector", lambda v: v.tensor_scalar_mul(smc(C_MRS), smc(C_MRS), -1.0), [], vA)
        EM("scalar", lambda s, bp=bp: s.activation(
            xnt[:, :], xt[bp][:, :], AF.Identity, bias=smc(C_MRS), scale=smc(C_RSTD)),
            [(vA, vA.count), (tA, tA.count)], aA)
        a_lnapply = aA.count
        for c in range(32):
            pp = c % 2
            EM("tensor", lambda t, c=c, pp=pp: t.transpose(
                pb_t[pp][:, 0:128], xnt[:, c * 128:(c + 1) * 128], sb_ident[:, :]),
                ([(aA, a_lnapply), (s_ms, s_const)] if c == 0 else [])
                + [(vA, v_tr_done[pp])], tA)
            t_tr_done[pp] = tA.count
            EM("vector", lambda v, c=c, pp=pp, tt=tt: v.tensor_copy(
                xn_d[:, c * T + tt * 128: c * T + tt * 128 + 128], pb_t[pp][:, 0:128]),
                [(tA, t_tr_done[pp])], vA)
            v_tr_done[pp] = vA.count
    v_xn = vA.count

    if DEBUG_OUTS:
        DMA("sync", lambda s: s.dma_start(out=dbg_xn, in_=xn_d[:, :]), [(vA, v_xn)], s_ms)

    t_kproj = {}
    a_kevac = {0: 0, 1: 0}
    a_ksq_head = {}
    v_knorm = 0
    s_kvw_head = {}
    for h in range(H):
        bp = h % 2
        w = [(tA, t_kproj[h - 2])] if h >= 2 else [(vA, v_xn), (aA, a_lnapply)]
        DMA("gpsimd", lambda g, h=h, bp=bp: g.dma_start(
            out=wkb[bp][:, :], in_=wk_t[h].rearrange("a b c -> b a c")), w, s_w)
        w_w = (s_w, s_w.count)
        for th in range(2):
            for dc in range(32):
                waits = []
                if dc == 0:
                    waits = [w_w, (vA, v_xn)]
                    if h >= 1:
                        waits.append((aA, a_kevac[th]))
                EM("tensor", lambda t, th=th, dc=dc, bp=bp: t.matmul(
                    pb_y[th][:, :], wkb[bp][:, dc * 128:(dc + 1) * 128],
                    xn_d[:, dc * T + th * 512:dc * T + (th + 1) * 512],
                    start=(dc == 0), stop=(dc == 31)), waits, tA)
            waits = [(tA, tA.count)]
            if h >= 2 and th == 0:
                waits += [(s_kvw, s_kvw.count), (aA, a_ksq_head[h - 2])]
            EM("scalar", lambda s, th=th, h=h, bp=bp: s.activation(
                kqb[bp][:, th * 512:(th + 1) * 512], pb_y[th][:, :], AF.Identity,
                bias=sb_bk[:, h:h + 1]), waits, aA)
            a_kevac[th] = aA.count
        t_kproj[h] = tA.count
        a_kq = aA.count
        EM("scalar", lambda s, bp=bp: s.activation(ksq[:, :], kqb[bp][:, :], AF.Square),
           [(vA, v_knorm)], aA)
        a_ksq_head[h] = aA.count
        a_sq = aA.count
        for th in range(2):
            EM("tensor", lambda t, th=th: t.matmul(
                pb_s[th][0:1, 0:512], sb_ones[:, 0:1], ksq[:, th * 512:(th + 1) * 512],
                start=True, stop=True),
                ([(aA, a_sq)] + ([(vA, v_knorm)] if h >= 1 else [])) if th == 0 else [], tA)
            EM("vector", lambda v, th=th: v.reduce_max(
                (smc(C_T0) if th == 0 else smc(C_T1))[0:1, :], pb_s[th][0:1, 0:512], axis=AX.X),
                [(tA, tA.count)], vA)
        EM("vector", lambda v: v.tensor_max(smc(C_U1)[0:1, :], smc(C_T0)[0:1, :], smc(C_T1)[0:1, :]), [], vA)
        EM("vector", lambda v, h=h: v.tensor_copy(nacc[0:1, h:h + 1], smc(C_U1)[0:1, :]), [], vA)
        v_knorm = vA.count
        DMA("sync", lambda s, h=h, bp=bp: s.dma_start(out=kview[h], in_=kqb[bp][:, :]),
            [(aA, a_kq)], s_kvw)
        s_kvw_head[h] = s_kvw.count
    DMA("sync", lambda s: s.dma_start(out=norms_row, in_=nacc[0:1, 0:H]), [(vA, v_knorm)], s_kvw)

    t_vmm = {0: 0, 1: 0}
    v_vevac = {0: 0, 1: 0}
    s_vdma = {0: 0, 1: 0}
    for hg in range(8):
        bw = hg % 2
        w = [(tA, t_vmm[bw])] if hg >= 2 else [(vA, v_xn)]
        DMA("gpsimd", lambda g, hg=hg, bw=bw: g.dma_start(
            out=wvb[bw][:, :], in_=wv_t[hg].rearrange("a b c -> b a c")), w, s_w)
        w_w = (s_w, s_w.count)
        for tt in range(8):
            pbv = pb_s[tt % 2]
            for dc in range(32):
                waits = []
                if dc == 0:
                    waits = [w_w, (vA, v_vevac[tt % 2])]
                    if hg == 0 and tt < 2:
                        waits.append((vA, v_knorm))
                EM("tensor", lambda t, tt=tt, dc=dc, bw=bw, pbv=pbv: t.matmul(
                    pbv[:, :], xn_d[:, dc * T + tt * 128:dc * T + tt * 128 + 128],
                    wvb[bw][:, dc * 512:(dc + 1) * 512],
                    start=(dc == 0), stop=False), waits, tA)
            EM("tensor", lambda t, hg=hg, pbv=pbv: t.matmul(
                pbv[:, :], sb_ones[:, :], sb_bvp[:, hg * 512:(hg + 1) * 512],
                start=False, stop=True), [], tA)
            t_vmm[bw] = tA.count
            EM("vector", lambda v, tt=tt, pbv=pbv: v.tensor_copy(vtb[tt % 2][:, :], pbv[:, :]),
               [(tA, tA.count), (s_kvw, s_kvw.count)], vA)
            v_vevac[tt % 2] = vA.count
            DMA("sync", lambda s, hg=hg, tt=tt: s.dma_start(
                out=vview[tt * 128:(tt + 1) * 128, hg * 512:(hg + 1) * 512],
                in_=vtb[tt % 2][:, :]), [(vA, vA.count)], s_kvw)
            s_vdma[tt % 2] = s_kvw.count
    t_p1, v_p1, a_p1 = tA.count, vA.count, aA.count

    EM("gpsimd", lambda g: g.collective_compute(
        "AllGather", ALU.bypass, replica_groups=[[0, 1, 2, 3], [4, 5, 6, 7]],
        ins=[kv_src.ap()], outs=[kv_gath.ap()]), [(s_kvw, s_kvw.count)], s_cc, 1)

    if DEBUG_OUTS:
        DMA("sync", lambda s: s.dma_start(out=dbg_kv, in_=kv_gath.ap()), [(s_cc, 1)], s_ms)

    p1_bar = [(tA, t_p1), (vA, v_p1), (aA, a_p1), (s_kvw, s_kvw.count), (s_w, s_w.count)]
    DMA("sync", lambda s: s.dma_start(out=sb_mask[:, :], in_=masks_in), p1_bar, s_ms)
    s_mask_ld = s_ms.count
    for b in range(2):
        for rs in range(32):
            EM("vector", lambda v, b=b, rs=rs: v.memset(
                vh[b][:, rs * VST + 128:rs * VST + 129], 1.0),
                p1_bar if (b == 0 and rs == 0) else [], vA)
    v_vones = vA.count
    DMA("sync", lambda s: s.dma_start(out=sb_nk[0:1, :], in_=gN,
        allow_slow_non_contiguous=True), [(s_cc, 1)], s_gr)
    s_nk = s_gr.count

    t_head = {}
    v_head = {}
    a_exphead = {}
    s_ot_head = {}
    v_stevac = {0: 0, 1: 0}
    a_stexp = {0: 0, 1: 0}
    v_oaug = {0: 0, 1: 0, 2: 0, 3: 0}
    v_otcopy = {0: 0, 1: 0}
    t_osb = 0
    v_mhb_prev = 0

    for h in range(H):
        bp = h % 2
        w = [(s_cc, 1)]
        if h >= 2:
            w += [(tA, t_head[h - 2])]
        for r4 in range(4):
            DMA("sync", lambda s, h=h, bp=bp, r4=r4: s.dma_start(
                out=kh[bp][:, r4 * 1024:(r4 + 1) * 1024], in_=gK[h][:, r4]),
                w if r4 == 0 else [], s_gr)
        for r4 in range(4):
            DMA("sync", lambda s, h=h, bp=bp, r4=r4: s.dma_start(
                out=vh[bp].ap().rearrange("p (s c) -> p s c", c=VST)[:, r4 * 8:(r4 + 1) * 8, 0:128],
                in_=gV[h][:, r4]),
                [(vA, v_vones)] if (h < 2 and r4 == 0) else [], s_gr)
        w_kv = (s_gr, s_gr.count)
        wv2 = [(tA, t_head[h - 2])] if h >= 2 else list(p1_bar)
        DMA("gpsimd", lambda g, h=h, bp=bp: g.dma_start(
            out=wqb[bp][:, :], in_=wq_t[h].rearrange("a b c -> b a c")), wv2, s_w)
        w_wq = (s_w, s_w.count)
        if h < 4:
            CROWS = E_W // 8 // 512 // 4
            DMA("gpsimd", lambda g, h=h: g.dma_start(
                out=ag_w_in.ap()[h * CROWS:(h + 1) * CROWS, :],
                in_=wslice[h * CROWS:(h + 1) * CROWS, :]), [], s_cv)
        if h == 3:
            EM("gpsimd", lambda g: g.collective_compute(
                "AllGather", ALU.bypass, replica_groups=[list(range(8))],
                ins=[ag_w_in.ap()], outs=[ag_w_out.ap()]),
                [(s_cv, s_cv.count)], s_cc, 1)

        for th in range(2):
            for dc in range(32):
                waits = []
                if dc == 0:
                    waits = [w_wq, (vA, v_xn)]
                    waits.append((aA, a_stexp[th]) if h >= 1 else (vA, v_p1))
                    if h == 0:
                        waits.append((vA, v_bq))
                EM("tensor", lambda t, th=th, dc=dc, bp=bp: t.matmul(
                    pb_s[th][:, :], wqb[bp][:, dc * 128:(dc + 1) * 128],
                    xn_d[:, dc * T + th * 512:dc * T + (th + 1) * 512],
                    start=(dc == 0), stop=(dc == 31)), waits, tA)
            waits = [(tA, tA.count)]
            if h >= 1 and th == 0:
                waits.append((aA, a_exphead[h - 1]))
            EM("scalar", lambda s, th=th, h=h: s.activation(
                qh[:, th * 512:(th + 1) * 512], pb_s[th][:, :], AF.Identity,
                bias=sb_bq[:, h:h + 1], scale=SCALE), waits, aA)
        EM("scalar", lambda s: s.activation(qsq[:, :], qh[:, :], AF.Square), [], aA)
        a_qsq = aA.count
        for th in range(2):
            EM("tensor", lambda t, th=th: t.matmul(
                pb_s[th][0:1, 0:512], sb_ones[:, 0:1], qsq[:, th * 512:(th + 1) * 512],
                start=True, stop=True),
                [(aA, a_qsq)] if th == 0 else [], tA)
            EM("vector", lambda v, th=th: v.reduce_max(
                (smc(C_T0) if th == 0 else smc(C_T1))[0:1, :], pb_s[th][0:1, 0:512], axis=AX.X),
                [(tA, tA.count)], vA)
        EM("vector", lambda v: v.tensor_max(smc(C_MQ)[0:1, :], smc(C_T0)[0:1, :], smc(C_T1)[0:1, :]), [], vA)
        EM("vector", lambda v, h=h: v.reduce_max(
            smc(C_MK)[0:1, :], sb_nk[0:1, h * 4:(h + 1) * 4], axis=AX.X),
            [(s_gr, s_gr.count)], vA)
        EM("vector", lambda v: v.tensor_mul(smc(C_T0)[0:1, :], smc(C_MQ)[0:1, :], smc(C_MK)[0:1, :]), [], vA)
        EM("scalar", lambda s: s.activation(smc(C_T1)[0:1, :], smc(C_T0)[0:1, :], AF.Ln), [(vA, vA.count)], aA)
        EM("scalar", lambda s: s.activation(smc(C_T0)[0:1, :], smc(C_T1)[0:1, :], AF.Exp, scale=0.5), [], aA)
        EM("vector", lambda v: v.tensor_scalar_mul(smc(C_U1)[0:1, :], smc(C_T0)[0:1, :], -1.0),
           [(aA, aA.count)] + ([(aA, a_exphead[h - 1])] if h >= 1 else []), vA)
        EM("vector", lambda v: v.tensor_copy(sb_bc[0:1, 0:1], smc(C_U1)[0:1, :]), [], vA)
        EM("tensor", lambda t: t.matmul(pb_s[0][:, 0:1], sb_ones[:, :], sb_bc[:, 0:1],
                                        start=True, stop=True), [(vA, vA.count)], tA)
        EM("vector", lambda v: v.tensor_copy(smc(C_MHB), pb_s[0][:, 0:1]), [(tA, tA.count)], vA)
        v_mhb = vA.count
        v_mhb_prev = v_mhb

        first_fin = True

        def emit_st(si):
            g, j, act, moff = SCHED[si]
            jp = si % 2
            rp = si % 4
            n = len(act)
            qlo, qhi = act[0] * 128, (act[-1] + 1) * 128
            r_, s_ = block_owner(j)
            pos = r_ * 8 + s_
            waits = [(aA, a_stexp[jp])]
            if si == 0:
                waits += [w_kv, (s_ms, s_mask_ld), (vA, v_mhb)]
            EM("tensor", lambda t, jp=jp, pos=pos, qlo=qlo, qhi=qhi, bp=bp: t.matmul(
                pb_s[jp][:, 0:qhi - qlo], kh[bp][:, pos * 128:(pos + 1) * 128],
                qh[:, qlo:qhi], start=True, stop=False), waits, tA)
            EM("tensor", lambda t, jp=jp, n=n, moff=moff: t.matmul(
                pb_s[jp][:, 0:n * 128], sb_ident[:, :],
                sb_mask[:, moff:moff + n * 128], start=False, stop=True), [], tA)
            t_st = tA.count
            EM("scalar", lambda s, jp=jp, rp=rp, n=n: s.activation(
                ptb[rp][:, 0:n * 128], pb_s[jp][:, 0:n * 128], AF.Exp, bias=smc(C_MHB)),
                [(tA, t_st)], aA)
            a_stexp[jp] = aA.count
            return si

        def emit_av(si):
            nonlocal first_fin, t_osb
            g, j, act, moff = SCHED[si]
            jp = si % 2
            rp = si % 4
            r_, s_ = block_owner(j)
            pos = r_ * 8 + s_
            for idx, sl in enumerate(act):
                c_in = sl % 4
                first, last = (j == 0), (j == PADS[sl] - 1)
                avw = [(aA, a_stexp[jp])] if idx == 0 else []
                if first:
                    avw.append((vA, v_oaug[c_in]))
                EM("tensor", lambda t, c_in=c_in, idx=idx, rp=rp, pos=pos, bp=bp,
                   first=first, last=last: t.matmul(
                    pb_y[c_in][:, 0:129], ptb[rp][:, idx * 128:(idx + 1) * 128],
                    vh[bp][:, pos * VST:pos * VST + 129],
                    start=first, stop=last), avw, tA)
                if last:
                    tb = sl % 2
                    EM("vector", lambda v, c_in=c_in: v.reciprocal(
                        smc(C_INVL), pb_y[c_in][:, 128:129]), [(tA, tA.count)], vA)
                    EM("vector", lambda v, c_in=c_in: v.tensor_scalar(
                        osb[:, :], pb_y[c_in][:, 0:128], smc(C_INVL), None, ALU.mult),
                        [(tA, t_osb)], vA)
                    v_oaug[c_in] = vA.count
                    EM("tensor", lambda t, tb=tb: t.transpose(
                        pb_t[tb][:, 0:128], osb[:, :], sb_ident[:, :]),
                        [(vA, v_oaug[c_in]), (vA, v_otcopy[tb])], tA)
                    t_osb = tA.count
                    cw = [(tA, t_osb)]
                    if first_fin and h >= 2:
                        cw.append((s_ot, s_ot.count))
                    first_fin = False
                    EM("vector", lambda v, sl=sl, bp=bp, tb=tb: v.tensor_copy(
                        oth[bp][:, sl * 128:(sl + 1) * 128], pb_t[tb][:, 0:128]), cw, vA)
                    v_otcopy[tb] = vA.count

        pend2 = []
        for si in range(len(SCHED)):
            emit_st(si)
            if len(pend2) == 2:
                emit_av(pend2.pop(0))
            pend2.append(si)
        for si in pend2:
            emit_av(si)
        a_exphead[h] = aA.count
        DMA("sync", lambda s, h=h, bp=bp: s.dma_start(
            out=ot_dram.ap()[h * 128:(h + 1) * 128, :], in_=oth[bp][:, :]),
            [(vA, vA.count)], s_ot)
        s_ot_head[h] = s_ot.count
        t_head[h] = tA.count
        v_head[h] = vA.count

    DMA("sync", lambda s: s.dma_start(out=xn_spill.ap(), in_=xn_d[:, :]),
        [(tA, tA.count)], s_sp)
    s_spill = s_sp.count
    if DEBUG_OUTS:
        DMA("sync", lambda s: s.dma_start(out=dbg_ot, in_=ot_dram.ap()),
            [(s_ot, s_ot.count)], s_ms)

    t_p3, v_p3, a_p3 = tA.count, vA.count, aA.count
    s_gr_p3, s_ot_p3, s_w_p3 = s_gr.count, s_ot.count, s_w.count

    BAR = [(tA, t_p3), (vA, v_p3), (aA, a_p3), (s_sp, s_spill),
           (s_ot, s_ot_p3), (s_gr, s_gr_p3), (s_w, s_w_p3), (s_ms, s_ms.count),
           (s_cc, 2), (s_cv, s_cv.count)]

    t_half_end = 0
    t_gate = {0: 0, 1: 0}
    a_silu = 0
    v_hmul = 0
    t_wd = {0: 0, 1: 0}
    t_wo = {0: 0, 1: 0}
    v_yevac = {0: 0, 1: 0, 2: 0, 3: 0}
    s_orb = {0: 0, 1: 0}
    v_orb = {0: 0, 1: 0}

    v_b_end = 0
    s_out_b = 0
    for th in range(2):
        hb = list(BAR) if th == 0 else [(tA, t_half_end), (vA, v_b_end), (s_out, s_out_b)]
        DMA("sync", lambda s, th=th: s.dma_start(
            out=xnh[:, :],
            in_=xn_spill.ap().rearrange("p (d t) -> p d t", d=32)[:, :, th * 512:(th + 1) * 512]),
            hb + [(s_sp, s_spill)], s_sp)
        w_xn = (s_sp, s_sp.count)
        DMA("sync", lambda s, th=th: s.dma_start(
            out=otx[:, :],
            in_=ot_dram.ap().rearrange("(ic p) t -> p ic t", p=128)[:, :, th * 512:(th + 1) * 512]),
            hb, s_ot)
        w_ot = (s_ot, s_ot.count)

        for fb in range(NFB):
            bpf = fb % 2
            if fb < 2:
                w = list(hb)
            else:
                w = [(tA, t_gate[bpf])]
            DMA("sync", lambda s, fb=fb, bpf=bpf: s.dma_start(
                out=wgb[bpf][:, :], in_=wg_view[fb]), w, s_kvw)
            DMA("scalar", lambda s, fb=fb, bpf=bpf: s.dma_start(
                out=wub[bpf][:, :], in_=wu_view[fb]), list(w), s_wu)
            w_wf = (s_kvw, s_kvw.count)
            w_wf2 = (s_wu, s_wu.count)
            for dc in range(32):
                waits = [w_wf, w_wf2, w_xn, (aA, a_silu)] if dc == 0 else []
                EM("tensor", lambda t, dc=dc, bpf=bpf: t.matmul(
                    pb_s[0][:, :], wgb[bpf][:, dc * 128:(dc + 1) * 128],
                    xnh[:, dc * 512:(dc + 1) * 512],
                    start=(dc == 0), stop=(dc == 31)), waits, tA)
            t_g = tA.count
            for dc in range(32):
                waits = [(vA, v_hmul)] if dc == 0 else []
                EM("tensor", lambda t, dc=dc, bpf=bpf: t.matmul(
                    pb_s[1][:, :], wub[bpf][:, dc * 128:(dc + 1) * 128],
                    xnh[:, dc * 512:(dc + 1) * 512],
                    start=(dc == 0), stop=(dc == 31)), waits, tA)
            t_gate[bpf] = tA.count
            EM("scalar", lambda s, bpf=bpf: s.activation(
                gsb[bpf][:, :], pb_s[0][:, :], AF.Silu),
                [(tA, t_g), (vA, v_hmul)], aA)
            a_silu = aA.count
            EM("vector", lambda v, fb=fb, bpf=bpf: v.tensor_tensor(
                h_sb[:, fb * 512:(fb + 1) * 512], pb_s[1][:, :], gsb[bpf][:, :], ALU.mult),
                [(tA, t_gate[bpf]), (aA, a_silu)], vA)
            v_hmul = vA.count
        v_h_done = vA.count
        t_a_end = tA.count

        for do in range(8):
            w_wd = None
            for fb in range(NFB):
                kb = fb // 8
                bwd = kb % 2
                if fb % 8 == 0:
                    lo, hi = kb * 8, min(NFB, kb * 8 + 8)
                    dw = [(tA, t_wd[bwd])]
                    if do == 0 and kb == 0:
                        dw += [w_ot, (vA, v_h_done), (aA, a_silu), (tA, t_a_end)]
                    DMA("sync", lambda s, do=do, lo=lo, hi=hi, bwd=bwd: s.dma_start(
                        out=wdb[bwd][:, 0:(hi - lo) * 512],
                        in_=wd_view[do][:, lo:hi, :]), dw, s_kvw)
                    w_wd = (s_kvw, s_kvw.count)
                for tt in range(4):
                    waits = []
                    if fb % 8 == 0 and tt == 0:
                        waits = [w_wd, (vA, v_h_done)]
                        if fb == 0:
                            waits.append((vA, v_yevac[3]))
                    EM("tensor", lambda t, fb=fb, tt=tt, bwd=bwd: t.matmul(
                        pb_y[tt][:, :], h_sb[:, fb * 512 + tt * 128:fb * 512 + tt * 128 + 128],
                        wdb[bwd][:, (fb % 8) * 512:(fb % 8 + 1) * 512],
                        start=(fb == 0), stop=False), waits, tA)
                if fb % 8 == 7 or fb == NFB - 1:
                    t_wd[bwd] = tA.count
            for ic in range(32):
                kb = ic // 4
                bwo = kb % 2
                if ic % 4 == 0:
                    dw = [(tA, t_wo[bwo])]
                    if do == 0 and kb == 0:
                        dw += [(aA, a_silu), (tA, t_a_end)]
                    DMA("scalar", lambda s, do=do, kb=kb, bwo=bwo: s.dma_start(
                        out=wob[bwo][:, :], in_=wo_view[do][:, kb * 4:(kb + 1) * 4, :]), dw, s_wu)
                    w_wo = (s_wu, s_wu.count)
                for tt in range(4):
                    waits = [w_wo] if (ic % 4 == 0 and tt == 0) else []
                    EM("tensor", lambda t, ic=ic, tt=tt, bwo=bwo: t.matmul(
                        pb_y[tt][:, :], otx[:, ic * 512 + tt * 128:ic * 512 + tt * 128 + 128],
                        wob[bwo][:, (ic % 4) * 512:(ic % 4 + 1) * 512],
                        start=False, stop=False), waits, tA)
                if ic % 4 == 3:
                    t_wo[bwo] = tA.count
            for tt in range(4):
                EM("tensor", lambda t, do=do, tt=tt: t.matmul(
                    pb_y[tt][:, :], sb_ones[:, :], sb_bop[:, do * 512:(do + 1) * 512],
                    start=False, stop=True), [], tA)
            t_bo = tA.count
            for tt in range(4):
                ob = tt % 2
                DMA("scalar", lambda s, th=th, do=do, tt=tt, ob=ob: s.dma_start(
                    out=xrb[ob][:, :],
                    in_=x_in[th * 512 + tt * 128:th * 512 + tt * 128 + 128,
                             do * 512:(do + 1) * 512]),
                    [(vA, v_orb[ob])] + ([(tA, t_a_end)] if (do == 0 and tt < 2) else []),
                    s_xr)
                EM("vector", lambda v, tt=tt, ob=ob: v.tensor_tensor(
                    orb[ob][:, :], pb_y[tt][:, :], xrb[ob][:, :], ALU.add),
                    [(tA, t_bo), (s_xr, s_xr.count), (s_out, s_out.count)], vA)
                v_yevac[tt] = vA.count
                v_orb[ob] = vA.count
                DMA("scalar", lambda s, th=th, do=do, tt=tt, ob=ob: s.dma_start(
                    out=out_dram[th * 512 + tt * 128:th * 512 + tt * 128 + 128,
                                 do * 512:(do + 1) * 512],
                    in_=orb[ob][:, :]), [(vA, vA.count)], s_out)
                s_orb[ob] = s_out.count
        t_half_end = tA.count
        v_b_end = vA.count
        s_out_b = s_out.count

    EM("scalar", None, [(s_out, s_out.count)])
    EM("sync", None, [(s_out, s_out.count)])
    EM("vector", None, [(s_out, s_out.count)])

    targets = rec.collect_targets()
    with nc.Block() as block:
        block.tensor(lambda engine: rec.replay("tensor", engine, targets))
        block.vector(lambda engine: rec.replay("vector", engine, targets))
        block.scalar(lambda engine: rec.replay("scalar", engine, targets))
        block.gpsimd(lambda engine: rec.replay("gpsimd", engine, targets))
        block.sync(lambda engine: rec.replay("sync", engine, targets))

    return nc, st



def _tile_w(w, cols):
    o, i = w.shape
    return np.ascontiguousarray(
        w.reshape(o // cols, cols, i // 128, 128).transpose(0, 2, 3, 1))


def build_masks(r):
    blks = rank_blocks(r)
    m = np.zeros((128, MASK_COLS), dtype=np.float32)
    kk = np.arange(128)[:, None]
    qq = np.arange(128)[None, :]
    tri = np.where(kk <= qq, 0.0, NEG).astype(np.float32)
    for (g, j, act, off) in SCHED:
        for idx, sl in enumerate(act):
            gq = blks[sl]
            col = off + idx * 128
            if j == gq:
                m[:, col:col + 128] = tri
            elif j > gq:
                m[:, col:col + 128] = NEG
    return m.astype(ml_dtypes.bfloat16)


_PROG_CACHE = {}


def _get_prog():
    if "p" not in _PROG_CACHE:
        _PROG_CACHE["p"] = build_program()
    return _PROG_CACHE["p"][0]


def kernel(L0_attn_x, L0_attn_Wq, L0_attn_bq, L0_attn_Wk, L0_attn_bk,
           L0_attn_Wv, L0_attn_bv, L0_attn_Wo, L0_attn_bo,
           L0_ffn_W_gate, L0_ffn_W_up, L0_ffn_W_down):
    x = np.asarray(L0_attn_x, np.float32)
    wq_t = _tile_w(np.asarray(L0_attn_Wq, np.float32), 128)
    wk_t = _tile_w(np.asarray(L0_attn_Wk, np.float32), 128)
    wv_t = _tile_w(np.asarray(L0_attn_Wv, np.float32), 512)
    wo_t = _tile_w(np.asarray(L0_attn_Wo, np.float32), 512)
    wg_t = _tile_w(np.asarray(L0_ffn_W_gate, np.float32), 128)
    wu_t = _tile_w(np.asarray(L0_ffn_W_up, np.float32), 128)
    wd_t = _tile_w(np.asarray(L0_ffn_W_down, np.float32), 512)
    wflat = np.concatenate([wg_t.ravel(), wu_t.ravel(), wd_t.ravel(), wo_t.ravel()])
    wflat = wflat.reshape(E_W // 512, 512)
    WR = E_W // 8 // 512
    bq_t = np.ascontiguousarray(np.asarray(L0_attn_bq, np.float32).reshape(32, 128).T)
    bk_t = np.ascontiguousarray(np.asarray(L0_attn_bk, np.float32).reshape(32, 128).T)
    bv_row = np.ascontiguousarray(np.asarray(L0_attn_bv, np.float32).reshape(1, D))
    bo_row = np.ascontiguousarray(np.asarray(L0_attn_bo, np.float32).reshape(1, D))
    ident = np.eye(128, dtype=ml_dtypes.bfloat16)

    nc = _get_prog()
    in_maps = []
    for c in range(8):
        b, r = c // 4, c % 4
        blks = rank_blocks(r)
        rows = np.concatenate([np.arange(j * 128, (j + 1) * 128) for j in blks])
        in_maps.append(dict(
            x_core=np.ascontiguousarray(x[b][rows]),
            wq_t=wq_t, wk_t=wk_t, wv_t=wv_t,
            wslice=np.ascontiguousarray(wflat[c * WR:(c + 1) * WR]),
            bq_t=bq_t, bk_t=bk_t, bv_row=bv_row, bo_row=bo_row,
            masks_t=build_masks(r), ident=ident,
        ))
    res = run_bass_kernel_spmd(nc, in_maps, core_ids=list(range(8)))
    out = np.empty((B, L, D), dtype=np.float32)
    for c in range(8):
        b, r = c // 4, c % 4
        blks = rank_blocks(r)
        oc = res.results[c]["out_core"]
        for sl, j in enumerate(blks):
            out[b, j * 128:(j + 1) * 128, :] = oc[sl * 128:(sl + 1) * 128, :]
    if DEBUG_OUTS:
        kernel.last_results = res
    return out


# version 10
# speedup vs baseline: 1.1547x; 1.1547x over previous
import os
os.environ.setdefault("NEURON_SCRATCHPAD_PAGE_SIZE", "512")

import numpy as np
import ml_dtypes
from contextlib import ExitStack

import concourse.bass as bass
import concourse.mybir as mybir
from concourse.bass_utils import run_bass_kernel_spmd

F32 = mybir.dt.float32
BF16 = mybir.dt.bfloat16
AF = mybir.ActivationFunctionType
ALU = mybir.AluOpType
AX = mybir.AxisListType

B, L, D, H = 2, 4096, 4096, 32
DFF, NFB = 11008, 86
T = 1024
PADS = [4, 8, 12, 16, 20, 24, 28, 32]
NEG = -1.0e9
SCALE = float(1.0 / np.sqrt(128.0))
VST = 144

SZ_WG = NFB * 32 * 128 * 128
SZ_WD = 8 * NFB * 128 * 512
SZ_WO = 8 * 32 * 128 * 512
E_W = 2 * SZ_WG + SZ_WD + SZ_WO
OFF_WU, OFF_WD, OFF_WO = SZ_WG, 2 * SZ_WG, 2 * SZ_WG + SZ_WD

DEBUG_OUTS = False


def rank_blocks(r):
    return [r, 7 - r, 8 + r, 15 - r, 16 + r, 23 - r, 24 + r, 31 - r]


def block_owner(j):
    m, u = divmod(j, 8)
    return (u, 2 * m) if u < 4 else (7 - u, 2 * m + 1)


def att_schedule():
    sched, off = [], 0
    for g in range(2):
        slots = [4 * g + i for i in range(4)]
        for j in range(PADS[slots[-1]]):
            act = [s for s in slots if PADS[s] > j]
            sched.append((g, j, act, off))
            off += 128 * len(act)
    return sched


SCHED = att_schedule()
MASK_COLS = 144 * 128


class Sem:
    def __init__(self, handle, name):
        self.h, self.name, self.count = handle, name, 0


class Rec:
    ENGINES = ("tensor", "vector", "scalar", "gpsimd", "sync")

    def __init__(self):
        self.q = {e: [] for e in self.ENGINES}

    def emit(self, eng, fn, waits=(), inc=None, inc_n=1):
        self.q[eng].append((list(waits), fn, inc, inc_n))
        if inc is not None:
            inc.count += inc_n

    def collect_targets(self):
        t = {}
        for q in self.q.values():
            for waits, _, _, _ in q:
                for s, v in waits:
                    t.setdefault(id(s), set()).add(v)
        return t

    def replay(self, name, engobj, targets):
        cum = {}
        pending = {}
        for waits, fn, inc, inc_n in self.q[name]:
            merged = {}
            for s, v in waits:
                if v > 0 and v > merged.get(id(s), (None, -1))[1]:
                    merged[id(s)] = (s, v)
            for s, v in merged.values():
                engobj.wait_ge(s.h, v)
            if fn is not None:
                inst = fn(engobj)
                if inc is not None:
                    if inc_n == 16:
                        inst.then_inc(inc.h, inc_n)
                        continue
                    c = cum.get(id(inc), 0) + inc_n
                    cum[id(inc)] = c
                    p = pending.get(id(inc), 0) + inc_n
                    if c in targets.get(id(inc), ()):
                        inst.then_inc(inc.h, p)
                        pending[id(inc)] = 0
                    else:
                        pending[id(inc)] = p


def build_program():
    nc = bass.Bass("TRN2", target_bir_lowering=False)
    dt = nc.dram_tensor

    x_in = dt("x_core", [T, D], F32, kind="ExternalInput").ap()
    wq_t = dt("wq_t", [H, 32, 128, 128], F32, kind="ExternalInput").ap()
    wk_t = dt("wk_t", [H, 32, 128, 128], F32, kind="ExternalInput").ap()
    wv_t = dt("wv_t", [8, 32, 128, 512], F32, kind="ExternalInput").ap()
    wslice = dt("wslice", [E_W // 8 // 512, 512], F32, kind="ExternalInput").ap()
    bq_in = dt("bq_t", [128, 32], F32, kind="ExternalInput").ap()
    bk_in = dt("bk_t", [128, 32], F32, kind="ExternalInput").ap()
    bv_in = dt("bv_row", [1, D], F32, kind="ExternalInput").ap()
    bo_in = dt("bo_row", [1, D], F32, kind="ExternalInput").ap()
    masks_in = dt("masks_t", [128, MASK_COLS], BF16, kind="ExternalInput").ap()
    ident_in = dt("ident", [128, 128], BF16, kind="ExternalInput").ap()
    out_dram = dt("out_core", [T, D], F32, kind="ExternalOutput").ap()
    if DEBUG_OUTS:
        dbg_xn = dt("dbg_xn", [128, 32 * T], BF16, kind="ExternalOutput").ap()
        dbg_kv = dt("dbg_kv", [4 * 16400, 512], BF16, kind="ExternalOutput").ap()
        dbg_ot = dt("dbg_ot", [D, T], BF16, kind="ExternalOutput").ap()

    KVROWS = 16400
    kv_src = dt("kv_src", [KVROWS, 512], BF16, kind="Internal")
    kv_gath = dt("kv_gath", [4 * KVROWS, 512], BF16, kind="Internal")
    ot_dram = dt("ot_dram", [D, T], BF16, kind="Internal")
    xn_spill = dt("xn_spill", [128, 32 * T], BF16, kind="Internal")

    KSZ = H * 128 * T
    kview = kv_src.ap().rearrange("a b -> (a b)")[0:KSZ] \
        .rearrange("(h p t) -> h p t", h=H, p=128)
    vview = kv_src.ap().rearrange("a b -> (a b)")[KSZ:2 * KSZ] \
        .rearrange("(t d) -> t d", t=T)
    norms_row = kv_src.ap()[16384:16385, 0:H]

    gsp = kv_gath.ap().rearrange("a b -> (a b)").rearrange("(r z) -> r z", r=4)
    gK = gsp[:, 0:KSZ].rearrange("r (h p s t) -> h p r s t", h=H, p=128, s=8)
    gV = gsp[:, KSZ:2 * KSZ].rearrange("r (s t h c) -> h t r s c", s=8, t=128, h=H)
    gN = gsp[:, 16384 * 512:16384 * 512 + H].rearrange("r h -> h r")

    ag_w_in = dt("ag_w_in", [E_W // 8 // 512, 512], BF16, kind="Internal")
    ag_w_out = dt("ag_w_out", [E_W // 512, 512], BF16, kind="Internal", addr_space="Shared")
    wff = ag_w_out.ap().rearrange("a b -> (a b)")
    wg_view = wff[0:SZ_WG].rearrange("(fb dc p f) -> fb p dc f", fb=NFB, dc=32, p=128)
    wu_view = wff[OFF_WU:OFF_WU + SZ_WG].rearrange("(fb dc p f) -> fb p dc f", fb=NFB, dc=32, p=128)
    wd_view = wff[OFF_WD:OFF_WD + SZ_WD].rearrange("(do fb p f) -> do p fb f", do=8, fb=NFB, p=128)
    wo_view = wff[OFF_WO:OFF_WO + SZ_WO].rearrange("(do ic p f) -> do p ic f", do=8, ic=32, p=128)

    st = ExitStack()
    e = st.enter_context

    def sem(name):
        return Sem(e(nc.semaphore(name)), name)

    s_x = sem("s_x")
    s_w = sem("s_w")
    s_kvw = sem("s_kvw")
    s_gr = sem("s_gr")
    s_ot = sem("s_ot")
    s_sp = sem("s_sp")
    s_ms = sem("s_ms")
    s_xr = sem("s_xr")
    s_out = sem("s_out")
    s_cc = sem("s_cc")
    s_cv = sem("s_cv")
    s_wu = sem("s_wu")
    tA, vA, aA, gA = sem("tA"), sem("vA"), sem("aA"), sem("gA")

    rec = Rec()

    def EM(eng, fn, waits=(), inc=None, inc_n=1):
        waits = list(waits)
        if fn is not None:
            if eng == "vector":
                waits.append((vA, vA.count))
            elif eng == "scalar":
                waits.append((aA, aA.count))
        rec.emit(eng, fn, waits, inc, inc_n)

    def DMA(eng, fn, waits, s):
        rec.emit(eng, fn, waits, inc=s, inc_n=16)

    sb_ident = e(nc.sbuf_tensor("sb_ident", [128, 128], BF16))
    sb_ones = e(nc.sbuf_tensor("sb_ones", [128, 128], BF16))
    sb_bq = e(nc.sbuf_tensor("sb_bq", [128, 32], F32))
    sb_bk = e(nc.sbuf_tensor("sb_bk", [128, 32], F32))
    sb_bvp = e(nc.sbuf_tensor("sb_bvp", [128, D], BF16))
    sb_bop = e(nc.sbuf_tensor("sb_bop", [128, D], BF16))
    sb_sm = e(nc.sbuf_tensor("sb_sm", [128, 16], F32))
    sb_bc = e(nc.sbuf_tensor("sb_bc", [128, 16], BF16))
    sb_nk = e(nc.sbuf_tensor("sb_nk", [1, 128], BF16))
    (C_S1, C_S2, C_MEAN, C_VAR, C_STD, C_RSTD, C_MRS, C_M2,
     C_MQ, C_MK, C_T0, C_T1, C_INVL, C_MHB, C_U1, C_U2) = range(16)

    def smc(i):
        return sb_sm[:, i:i + 1]

    base = (nc.sbuf_base + 63) & ~63
    top = nc.sbuf_top

    def at(name, shape, dtype, off):
        nbytes = shape[1] * (2 if dtype == BF16 else 4)
        assert off + nbytes <= top, f"{name}: {off}+{nbytes} > {top}"
        return nc.alloc_sbuf_tensor_at(name, shape, dtype, offset=off)

    xn_d = at("xn_d", [128, 32 * T], BF16, base)
    SUB = base + 65536
    xt = [at(f"xt{i}", [128, D], F32, SUB + i * 16384) for i in range(2)]
    xsq = at("xsq", [128, D], F32, SUB + 32768)
    xnt = at("xnt", [128, D], BF16, SUB + 49152)
    wkb = [at(f"wkb{i}", [128, 32 * 128], BF16, SUB + i * 8192) for i in range(2)]
    kqb = [at(f"kqb{i}", [128, T], BF16, SUB + 16384 + i * 2048) for i in range(2)]
    ksq = at("ksq", [128, T], BF16, SUB + 20480)
    nacc = at("nacc", [1, 32], BF16, SUB + 22528)
    wvb = [at(f"wvb{i}", [128, 32 * 512], BF16, SUB + 23040 + i * 32768) for i in range(2)]
    vtb = [at(f"vtb{i}", [128, 512], BF16, SUB + 88576 + i * 1024) for i in range(2)]
    sb_mask = at("sb_mask", [128, MASK_COLS], BF16, SUB)
    kh = [at(f"kh{i}", [128, 32 * 128], BF16, SUB + 36864 + i * 8192) for i in range(2)]
    vh = [at(f"vh{i}", [128, 32 * VST], BF16, SUB + 53248 + i * 9216) for i in range(2)]
    wqb = [at(f"wqb{i}", [128, 32 * 128], BF16, SUB + 71680 + i * 8192) for i in range(2)]
    qh = at("qh", [128, T], BF16, SUB + 90624)
    qsq = at("qsq", [128, T], BF16, SUB + 92672)
    ptb = [at(f"ptb{i}", [128, 512], BF16, SUB + 96768 + i * 1024) for i in range(4)]
    osb = at("osb", [128, 128], BF16, SUB + 100864)
    oth = [at(f"oth{i}", [128, T], BF16, SUB + 101120 + i * 2048) for i in range(2)]
    assert SUB + 105216 <= top, SUB + 105216

    h_sb = at("h_sb", [128, NFB * 512], BF16, base)
    xnh = at("xnh", [128, 32 * 512], BF16, base + 88064)
    otx = at("otx", [128, 32 * 512], BF16, base + 120832)
    PB = base + 153600
    wgb = [at(f"wgb{i}", [128, 32 * 128], BF16, PB + i * 8192) for i in range(2)]
    wub = [at(f"wub{i}", [128, 32 * 128], BF16, PB + 16384 + i * 8192) for i in range(2)]
    gsb = [at(f"gsb{i}", [128, 512], BF16, PB + 32768 + i * 1024) for i in range(2)]
    wdb = [at(f"wdb{i}", [128, 8 * 512], BF16, PB + i * 8192) for i in range(2)]
    wob = [at(f"wob{i}", [128, 4 * 512], BF16, PB + 16384 + i * 4096) for i in range(2)]
    xrb = [at(f"xrb{i}", [128, 512], F32, PB + 24576 + i * 2048) for i in range(2)]
    orb = [at(f"orb{i}", [128, 512], F32, PB + 28672 + i * 2048) for i in range(2)]
    assert PB + 34816 <= top

    pb_y = [e(nc.psum_tensor(f"pb_y{i}", [128, 512], F32)) for i in range(4)]
    pb_s = [e(nc.psum_tensor(f"pb_s{i}", [128, 512], F32)) for i in range(2)]
    pb_t = [e(nc.psum_tensor(f"pb_t{i}", [128, 256], BF16)) for i in range(2)]

    DMA("sync", lambda s: s.dma_start(out=sb_ident[:, :], in_=ident_in), [], s_ms)
    DMA("sync", lambda s: s.dma_start(out=sb_bq[:, :], in_=bq_in), [], s_ms)
    DMA("sync", lambda s: s.dma_start(out=sb_bk[:, :], in_=bk_in), [], s_ms)
    s_const = s_ms.count
    EM("gpsimd", lambda g: g.memset(sb_ones[:, :], 1.0), [], gA)
    EM("gpsimd", lambda g: g.memset(sb_bvp[:, :], 0.0), [], gA)
    EM("gpsimd", lambda g: g.memset(sb_bop[:, :], 0.0), [], gA)
    EM("gpsimd", lambda g: g.memset(sb_bc[:, :], 0.0), [], gA)
    g_const = gA.count
    DMA("gpsimd", lambda g: g.dma_start(out=sb_bvp[0:1, :], in_=bv_in), [(gA, g_const)], s_w)
    DMA("gpsimd", lambda g: g.dma_start(out=sb_bop[0:1, :], in_=bo_in), [(gA, g_const)], s_w)
    EM("vector", lambda v: v.tensor_scalar_mul(sb_bq[:, :], sb_bq[:, :], SCALE),
       [(s_ms, s_const)], vA)
    v_bq = vA.count

    a_lnapply = 0
    t_tr_done = {0: 0, 1: 0}
    v_tr_done = {0: 0, 1: 0}
    for tt in range(8):
        bp = tt % 2
        w = [(aA, a_lnapply)] if tt >= 2 else []
        DMA("sync", lambda s, tt=tt, bp=bp: s.dma_start(
            out=xt[bp][:, :], in_=x_in[tt * 128:(tt + 1) * 128, :]), w, s_x)
        wx = [(s_x, s_x.count)]
        EM("vector", lambda v, bp=bp: v.reduce_sum(smc(C_S1), xt[bp][:, :], axis=AX.X), wx, vA)
        EM("scalar", lambda s, bp=bp: s.activation(xsq[:, :], xt[bp][:, :], AF.Square),
           wx + [(vA, vA.count)], aA)
        EM("vector", lambda v: v.reduce_sum(smc(C_S2), xsq[:, :], axis=AX.X), [(aA, aA.count)], vA)
        EM("vector", lambda v: v.tensor_scalar_mul(smc(C_MEAN), smc(C_S1), 1.0 / D), [], vA)
        EM("vector", lambda v: v.tensor_mul(smc(C_M2), smc(C_MEAN), smc(C_MEAN)), [], vA)
        EM("vector", lambda v: v.tensor_scalar(smc(C_VAR), smc(C_S2), 1.0 / D, None, ALU.mult),
           [(aA, a_lnapply)], vA)
        EM("vector", lambda v: v.tensor_sub(smc(C_VAR), smc(C_VAR), smc(C_M2)), [], vA)
        EM("vector", lambda v: v.tensor_scalar_add(smc(C_VAR), smc(C_VAR), 1e-5), [], vA)
        EM("scalar", lambda s: s.activation(smc(C_STD), smc(C_VAR), AF.Sqrt), [(vA, vA.count)], aA)
        EM("vector", lambda v: v.reciprocal(smc(C_RSTD), smc(C_STD)),
           [(aA, aA.count)], vA)
        EM("vector", lambda v: v.tensor_mul(smc(C_MRS), smc(C_MEAN), smc(C_RSTD)), [], vA)
        EM("vector", lambda v: v.tensor_scalar_mul(smc(C_MRS), smc(C_MRS), -1.0), [], vA)
        EM("scalar", lambda s, bp=bp: s.activation(
            xnt[:, :], xt[bp][:, :], AF.Identity, bias=smc(C_MRS), scale=smc(C_RSTD)),
            [(vA, vA.count), (tA, tA.count)], aA)
        a_lnapply = aA.count
        for c in range(32):
            pp = c % 2
            EM("tensor", lambda t, c=c, pp=pp: t.transpose(
                pb_t[pp][:, 0:128], xnt[:, c * 128:(c + 1) * 128], sb_ident[:, :]),
                ([(aA, a_lnapply), (s_ms, s_const)] if c == 0 else [])
                + [(vA, v_tr_done[pp])], tA)
            t_tr_done[pp] = tA.count
            EM("vector", lambda v, c=c, pp=pp, tt=tt: v.tensor_copy(
                xn_d[:, c * T + tt * 128: c * T + tt * 128 + 128], pb_t[pp][:, 0:128]),
                [(tA, t_tr_done[pp])], vA)
            v_tr_done[pp] = vA.count
    v_xn = vA.count

    if DEBUG_OUTS:
        DMA("sync", lambda s: s.dma_start(out=dbg_xn, in_=xn_d[:, :]), [(vA, v_xn)], s_ms)

    t_kproj = {}
    a_kevac = {0: 0, 1: 0}
    a_ksq_head = {}
    v_knorm = 0
    s_kvw_head = {}
    for h in range(H):
        bp = h % 2
        w = [(tA, t_kproj[h - 2])] if h >= 2 else [(vA, v_xn), (aA, a_lnapply)]
        DMA("gpsimd", lambda g, h=h, bp=bp: g.dma_start(
            out=wkb[bp][:, :], in_=wk_t[h].rearrange("a b c -> b a c")), w, s_w)
        w_w = (s_w, s_w.count)
        for th in range(2):
            for dc in range(32):
                waits = []
                if dc == 0:
                    waits = [w_w, (vA, v_xn)]
                    if h >= 1:
                        waits.append((aA, a_kevac[th]))
                EM("tensor", lambda t, th=th, dc=dc, bp=bp: t.matmul(
                    pb_y[th][:, :], wkb[bp][:, dc * 128:(dc + 1) * 128],
                    xn_d[:, dc * T + th * 512:dc * T + (th + 1) * 512],
                    start=(dc == 0), stop=(dc == 31)), waits, tA)
            waits = [(tA, tA.count)]
            if h >= 2 and th == 0:
                waits += [(s_kvw, s_kvw.count), (aA, a_ksq_head[h - 2])]
            EM("scalar", lambda s, th=th, h=h, bp=bp: s.activation(
                kqb[bp][:, th * 512:(th + 1) * 512], pb_y[th][:, :], AF.Identity,
                bias=sb_bk[:, h:h + 1]), waits, aA)
            a_kevac[th] = aA.count
        t_kproj[h] = tA.count
        a_kq = aA.count
        EM("scalar", lambda s, bp=bp: s.activation(ksq[:, :], kqb[bp][:, :], AF.Square),
           [(vA, v_knorm)], aA)
        a_ksq_head[h] = aA.count
        a_sq = aA.count
        for th in range(2):
            EM("tensor", lambda t, th=th: t.matmul(
                pb_s[th][0:1, 0:512], sb_ones[:, 0:1], ksq[:, th * 512:(th + 1) * 512],
                start=True, stop=True),
                ([(aA, a_sq)] + ([(vA, v_knorm)] if h >= 1 else [])) if th == 0 else [], tA)
            EM("vector", lambda v, th=th: v.reduce_max(
                (smc(C_T0) if th == 0 else smc(C_T1))[0:1, :], pb_s[th][0:1, 0:512], axis=AX.X),
                [(tA, tA.count)], vA)
        EM("vector", lambda v: v.tensor_max(smc(C_U1)[0:1, :], smc(C_T0)[0:1, :], smc(C_T1)[0:1, :]), [], vA)
        EM("vector", lambda v, h=h: v.tensor_copy(nacc[0:1, h:h + 1], smc(C_U1)[0:1, :]), [], vA)
        v_knorm = vA.count
        DMA("sync", lambda s, h=h, bp=bp: s.dma_start(out=kview[h], in_=kqb[bp][:, :]),
            [(aA, a_kq)], s_kvw)
        s_kvw_head[h] = s_kvw.count
    DMA("sync", lambda s: s.dma_start(out=norms_row, in_=nacc[0:1, 0:H]), [(vA, v_knorm)], s_kvw)

    t_vmm = {0: 0, 1: 0}
    v_vevac = {0: 0, 1: 0}
    s_vdma = {0: 0, 1: 0}
    for hg in range(8):
        bw = hg % 2
        w = [(tA, t_vmm[bw])] if hg >= 2 else [(vA, v_xn)]
        DMA("gpsimd", lambda g, hg=hg, bw=bw: g.dma_start(
            out=wvb[bw][:, :], in_=wv_t[hg].rearrange("a b c -> b a c")), w, s_w)
        w_w = (s_w, s_w.count)
        for tt in range(8):
            pbv = pb_s[tt % 2]
            for dc in range(32):
                waits = []
                if dc == 0:
                    waits = [w_w, (vA, v_vevac[tt % 2])]
                    if hg == 0 and tt < 2:
                        waits.append((vA, v_knorm))
                EM("tensor", lambda t, tt=tt, dc=dc, bw=bw, pbv=pbv: t.matmul(
                    pbv[:, :], xn_d[:, dc * T + tt * 128:dc * T + tt * 128 + 128],
                    wvb[bw][:, dc * 512:(dc + 1) * 512],
                    start=(dc == 0), stop=False), waits, tA)
            EM("tensor", lambda t, hg=hg, pbv=pbv: t.matmul(
                pbv[:, :], sb_ones[:, :], sb_bvp[:, hg * 512:(hg + 1) * 512],
                start=False, stop=True), [], tA)
            t_vmm[bw] = tA.count
            EM("vector", lambda v, tt=tt, pbv=pbv: v.tensor_copy(vtb[tt % 2][:, :], pbv[:, :]),
               [(tA, tA.count), (s_kvw, s_kvw.count)], vA)
            v_vevac[tt % 2] = vA.count
            DMA("sync", lambda s, hg=hg, tt=tt: s.dma_start(
                out=vview[tt * 128:(tt + 1) * 128, hg * 512:(hg + 1) * 512],
                in_=vtb[tt % 2][:, :]), [(vA, vA.count)], s_kvw)
            s_vdma[tt % 2] = s_kvw.count
    t_p1, v_p1, a_p1 = tA.count, vA.count, aA.count

    EM("gpsimd", lambda g: g.collective_compute(
        "AllGather", ALU.bypass, replica_groups=[[0, 1, 2, 3], [4, 5, 6, 7]],
        ins=[kv_src.ap()], outs=[kv_gath.ap()]), [(s_kvw, s_kvw.count)], s_cc, 1)

    if DEBUG_OUTS:
        DMA("sync", lambda s: s.dma_start(out=dbg_kv, in_=kv_gath.ap()), [(s_cc, 1)], s_ms)

    p1_bar = [(tA, t_p1), (vA, v_p1), (aA, a_p1), (s_kvw, s_kvw.count), (s_w, s_w.count)]
    DMA("sync", lambda s: s.dma_start(out=sb_mask[:, :], in_=masks_in), p1_bar, s_ms)
    s_mask_ld = s_ms.count
    for b in range(2):
        for rs in range(32):
            EM("vector", lambda v, b=b, rs=rs: v.memset(
                vh[b][:, rs * VST + 128:rs * VST + 129], 1.0),
                p1_bar if (b == 0 and rs == 0) else [], vA)
    v_vones = vA.count
    DMA("sync", lambda s: s.dma_start(out=sb_nk[0:1, :], in_=gN,
        allow_slow_non_contiguous=True), [(s_cc, 1)], s_gr)
    s_nk = s_gr.count

    t_head = {}
    v_head = {}
    a_exphead = {}
    s_ot_head = {}
    v_stevac = {0: 0, 1: 0}
    a_stexp = {0: 0, 1: 0}
    v_oaug = {0: 0, 1: 0, 2: 0, 3: 0}
    v_otcopy = {0: 0, 1: 0}
    t_osb = 0
    v_mhb_prev = 0

    for h in range(H):
        bp = h % 2
        w = [(s_cc, 1)]
        if h >= 2:
            w += [(tA, t_head[h - 2])]
        for r4 in range(4):
            DMA("sync", lambda s, h=h, bp=bp, r4=r4: s.dma_start(
                out=kh[bp][:, r4 * 1024:(r4 + 1) * 1024], in_=gK[h][:, r4]),
                w if r4 == 0 else [], s_gr)
        for r4 in range(4):
            DMA("sync", lambda s, h=h, bp=bp, r4=r4: s.dma_start(
                out=vh[bp].ap().rearrange("p (s c) -> p s c", c=VST)[:, r4 * 8:(r4 + 1) * 8, 0:128],
                in_=gV[h][:, r4]),
                [(vA, v_vones)] if (h < 2 and r4 == 0) else [], s_gr)
        w_kv = (s_gr, s_gr.count)
        wv2 = [(tA, t_head[h - 2])] if h >= 2 else list(p1_bar)
        DMA("gpsimd", lambda g, h=h, bp=bp: g.dma_start(
            out=wqb[bp][:, :], in_=wq_t[h].rearrange("a b c -> b a c")), wv2, s_w)
        w_wq = (s_w, s_w.count)
        if h < 4:
            CROWS = E_W // 8 // 512 // 4
            DMA("gpsimd", lambda g, h=h: g.dma_start(
                out=ag_w_in.ap()[h * CROWS:(h + 1) * CROWS, :],
                in_=wslice[h * CROWS:(h + 1) * CROWS, :]), [], s_cv)
        if h == 3:
            EM("gpsimd", lambda g: g.collective_compute(
                "AllGather", ALU.bypass, replica_groups=[list(range(8))],
                ins=[ag_w_in.ap()], outs=[ag_w_out.ap()]),
                [(s_cv, s_cv.count)], s_cc, 1)

        for th in range(2):
            for dc in range(32):
                waits = []
                if dc == 0:
                    waits = [w_wq, (vA, v_xn)]
                    waits.append((aA, a_stexp[th]) if h >= 1 else (vA, v_p1))
                    if h == 0:
                        waits.append((vA, v_bq))
                EM("tensor", lambda t, th=th, dc=dc, bp=bp: t.matmul(
                    pb_s[th][:, :], wqb[bp][:, dc * 128:(dc + 1) * 128],
                    xn_d[:, dc * T + th * 512:dc * T + (th + 1) * 512],
                    start=(dc == 0), stop=(dc == 31)), waits, tA)
            waits = [(tA, tA.count)]
            if h >= 1 and th == 0:
                waits.append((aA, a_exphead[h - 1]))
            EM("scalar", lambda s, th=th, h=h: s.activation(
                qh[:, th * 512:(th + 1) * 512], pb_s[th][:, :], AF.Identity,
                bias=sb_bq[:, h:h + 1], scale=SCALE), waits, aA)
        EM("scalar", lambda s: s.activation(qsq[:, :], qh[:, :], AF.Square), [], aA)
        a_qsq = aA.count
        for th in range(2):
            EM("tensor", lambda t, th=th: t.matmul(
                pb_s[th][0:1, 0:512], sb_ones[:, 0:1], qsq[:, th * 512:(th + 1) * 512],
                start=True, stop=True),
                [(aA, a_qsq)] if th == 0 else [], tA)
            EM("vector", lambda v, th=th: v.reduce_max(
                (smc(C_T0) if th == 0 else smc(C_T1))[0:1, :], pb_s[th][0:1, 0:512], axis=AX.X),
                [(tA, tA.count)], vA)
        EM("vector", lambda v: v.tensor_max(smc(C_MQ)[0:1, :], smc(C_T0)[0:1, :], smc(C_T1)[0:1, :]), [], vA)
        EM("vector", lambda v, h=h: v.reduce_max(
            smc(C_MK)[0:1, :], sb_nk[0:1, h * 4:(h + 1) * 4], axis=AX.X),
            [(s_gr, s_gr.count)], vA)
        EM("vector", lambda v: v.tensor_mul(smc(C_T0)[0:1, :], smc(C_MQ)[0:1, :], smc(C_MK)[0:1, :]), [], vA)
        EM("scalar", lambda s: s.activation(smc(C_T1)[0:1, :], smc(C_T0)[0:1, :], AF.Ln), [(vA, vA.count)], aA)
        EM("scalar", lambda s: s.activation(smc(C_T0)[0:1, :], smc(C_T1)[0:1, :], AF.Exp, scale=0.5), [], aA)
        EM("vector", lambda v: v.tensor_scalar_mul(smc(C_U1)[0:1, :], smc(C_T0)[0:1, :], -1.0),
           [(aA, aA.count)] + ([(aA, a_exphead[h - 1])] if h >= 1 else []), vA)
        EM("vector", lambda v: v.tensor_copy(sb_bc[0:1, 0:1], smc(C_U1)[0:1, :]), [], vA)
        EM("tensor", lambda t: t.matmul(pb_s[0][:, 0:1], sb_ones[:, :], sb_bc[:, 0:1],
                                        start=True, stop=True), [(vA, vA.count)], tA)
        EM("vector", lambda v: v.tensor_copy(smc(C_MHB), pb_s[0][:, 0:1]), [(tA, tA.count)], vA)
        v_mhb = vA.count
        v_mhb_prev = v_mhb

        first_fin = True
        a_exp_si = {}

        def emit_st(si):
            g, j, act, moff = SCHED[si]
            jp = si % 2
            rp = si % 4
            n = len(act)
            qlo, qhi = act[0] * 128, (act[-1] + 1) * 128
            r_, s_ = block_owner(j)
            pos = r_ * 8 + s_
            waits = [(aA, a_stexp[jp])]
            if si == 0:
                waits += [w_kv, (s_ms, s_mask_ld), (vA, v_mhb)]
            EM("tensor", lambda t, jp=jp, pos=pos, qlo=qlo, qhi=qhi, bp=bp: t.matmul(
                pb_s[jp][:, 0:qhi - qlo], kh[bp][:, pos * 128:(pos + 1) * 128],
                qh[:, qlo:qhi], start=True, stop=False), waits, tA)
            EM("tensor", lambda t, jp=jp, n=n, moff=moff: t.matmul(
                pb_s[jp][:, 0:n * 128], sb_ident[:, :],
                sb_mask[:, moff:moff + n * 128], start=False, stop=True), [], tA)
            t_st = tA.count
            EM("scalar", lambda s, jp=jp, rp=rp, n=n: s.activation(
                ptb[rp][:, 0:n * 128], pb_s[jp][:, 0:n * 128], AF.Exp, bias=smc(C_MHB)),
                [(tA, t_st)], aA)
            a_stexp[jp] = aA.count
            a_exp_si[si] = aA.count
            return si

        def emit_av(si):
            nonlocal first_fin, t_osb
            g, j, act, moff = SCHED[si]
            jp = si % 2
            rp = si % 4
            r_, s_ = block_owner(j)
            pos = r_ * 8 + s_
            for idx, sl in enumerate(act):
                c_in = sl % 4
                first, last = (j == 0), (j == PADS[sl] - 1)
                avw = [(aA, a_exp_si[si])] if idx == 0 else []
                if first:
                    avw.append((vA, v_oaug[c_in]))
                EM("tensor", lambda t, c_in=c_in, idx=idx, rp=rp, pos=pos, bp=bp,
                   first=first, last=last: t.matmul(
                    pb_y[c_in][:, 0:129], ptb[rp][:, idx * 128:(idx + 1) * 128],
                    vh[bp][:, pos * VST:pos * VST + 129],
                    start=first, stop=last), avw, tA)
                if last:
                    tb = sl % 2
                    EM("vector", lambda v, c_in=c_in: v.reciprocal(
                        smc(C_INVL), pb_y[c_in][:, 128:129]), [(tA, tA.count)], vA)
                    EM("vector", lambda v, c_in=c_in: v.tensor_scalar(
                        osb[:, :], pb_y[c_in][:, 0:128], smc(C_INVL), None, ALU.mult),
                        [(tA, t_osb)], vA)
                    v_oaug[c_in] = vA.count
                    EM("tensor", lambda t, tb=tb: t.transpose(
                        pb_t[tb][:, 0:128], osb[:, :], sb_ident[:, :]),
                        [(vA, v_oaug[c_in]), (vA, v_otcopy[tb])], tA)
                    t_osb = tA.count
                    cw = [(tA, t_osb)]
                    if first_fin and h >= 2:
                        cw.append((s_ot, s_ot.count))
                    first_fin = False
                    EM("vector", lambda v, sl=sl, bp=bp, tb=tb: v.tensor_copy(
                        oth[bp][:, sl * 128:(sl + 1) * 128], pb_t[tb][:, 0:128]), cw, vA)
                    v_otcopy[tb] = vA.count

        pend2 = []
        for si in range(len(SCHED)):
            emit_st(si)
            if len(pend2) == 2:
                emit_av(pend2.pop(0))
            pend2.append(si)
        for si in pend2:
            emit_av(si)
        a_exphead[h] = aA.count
        DMA("sync", lambda s, h=h, bp=bp: s.dma_start(
            out=ot_dram.ap()[h * 128:(h + 1) * 128, :], in_=oth[bp][:, :]),
            [(vA, vA.count)], s_ot)
        s_ot_head[h] = s_ot.count
        t_head[h] = tA.count
        v_head[h] = vA.count

    DMA("sync", lambda s: s.dma_start(out=xn_spill.ap(), in_=xn_d[:, :]),
        [(tA, tA.count)], s_sp)
    s_spill = s_sp.count
    if DEBUG_OUTS:
        DMA("sync", lambda s: s.dma_start(out=dbg_ot, in_=ot_dram.ap()),
            [(s_ot, s_ot.count)], s_ms)

    t_p3, v_p3, a_p3 = tA.count, vA.count, aA.count
    s_gr_p3, s_ot_p3, s_w_p3 = s_gr.count, s_ot.count, s_w.count

    BAR = [(tA, t_p3), (vA, v_p3), (aA, a_p3), (s_sp, s_spill),
           (s_ot, s_ot_p3), (s_gr, s_gr_p3), (s_w, s_w_p3), (s_ms, s_ms.count),
           (s_cc, 2), (s_cv, s_cv.count)]

    t_half_end = 0
    t_gate = {0: 0, 1: 0}
    a_silu = 0
    v_hmul = 0
    t_wd = {0: 0, 1: 0}
    t_wo = {0: 0, 1: 0}
    v_yevac = {0: 0, 1: 0, 2: 0, 3: 0}
    s_orb = {0: 0, 1: 0}
    v_orb = {0: 0, 1: 0}

    v_b_end = 0
    s_out_b = 0
    for th in range(2):
        hb = list(BAR) if th == 0 else [(tA, t_half_end), (vA, v_b_end), (s_out, s_out_b)]
        DMA("sync", lambda s, th=th: s.dma_start(
            out=xnh[:, :],
            in_=xn_spill.ap().rearrange("p (d t) -> p d t", d=32)[:, :, th * 512:(th + 1) * 512]),
            hb + [(s_sp, s_spill)], s_sp)
        w_xn = (s_sp, s_sp.count)
        DMA("sync", lambda s, th=th: s.dma_start(
            out=otx[:, :],
            in_=ot_dram.ap().rearrange("(ic p) t -> p ic t", p=128)[:, :, th * 512:(th + 1) * 512]),
            hb, s_ot)
        w_ot = (s_ot, s_ot.count)

        for fb in range(NFB):
            bpf = fb % 2
            if fb < 2:
                w = list(hb)
            else:
                w = [(tA, t_gate[bpf])]
            DMA("sync", lambda s, fb=fb, bpf=bpf: s.dma_start(
                out=wgb[bpf][:, :], in_=wg_view[fb]), w, s_kvw)
            DMA("scalar", lambda s, fb=fb, bpf=bpf: s.dma_start(
                out=wub[bpf][:, :], in_=wu_view[fb]), list(w), s_wu)
            w_wf = (s_kvw, s_kvw.count)
            w_wf2 = (s_wu, s_wu.count)
            for dc in range(32):
                waits = [w_wf, w_wf2, w_xn, (aA, a_silu)] if dc == 0 else []
                EM("tensor", lambda t, dc=dc, bpf=bpf: t.matmul(
                    pb_s[0][:, :], wgb[bpf][:, dc * 128:(dc + 1) * 128],
                    xnh[:, dc * 512:(dc + 1) * 512],
                    start=(dc == 0), stop=(dc == 31)), waits, tA)
            t_g = tA.count
            for dc in range(32):
                waits = [(vA, v_hmul)] if dc == 0 else []
                EM("tensor", lambda t, dc=dc, bpf=bpf: t.matmul(
                    pb_s[1][:, :], wub[bpf][:, dc * 128:(dc + 1) * 128],
                    xnh[:, dc * 512:(dc + 1) * 512],
                    start=(dc == 0), stop=(dc == 31)), waits, tA)
            t_gate[bpf] = tA.count
            EM("scalar", lambda s, bpf=bpf: s.activation(
                gsb[bpf][:, :], pb_s[0][:, :], AF.Silu),
                [(tA, t_g), (vA, v_hmul)], aA)
            a_silu = aA.count
            EM("vector", lambda v, fb=fb, bpf=bpf: v.tensor_tensor(
                h_sb[:, fb * 512:(fb + 1) * 512], pb_s[1][:, :], gsb[bpf][:, :], ALU.mult),
                [(tA, t_gate[bpf]), (aA, a_silu)], vA)
            v_hmul = vA.count
        v_h_done = vA.count
        t_a_end = tA.count

        for do in range(8):
            w_wd = None
            for fb in range(NFB):
                kb = fb // 8
                bwd = kb % 2
                if fb % 8 == 0:
                    lo, hi = kb * 8, min(NFB, kb * 8 + 8)
                    dw = [(tA, t_wd[bwd])]
                    if do == 0 and kb == 0:
                        dw += [w_ot, (vA, v_h_done), (aA, a_silu), (tA, t_a_end)]
                    DMA("sync", lambda s, do=do, lo=lo, hi=hi, bwd=bwd: s.dma_start(
                        out=wdb[bwd][:, 0:(hi - lo) * 512],
                        in_=wd_view[do][:, lo:hi, :]), dw, s_kvw)
                    w_wd = (s_kvw, s_kvw.count)
                for tt in range(4):
                    waits = []
                    if fb % 8 == 0 and tt == 0:
                        waits = [w_wd, (vA, v_h_done)]
                        if fb == 0:
                            waits.append((vA, v_yevac[3]))
                    EM("tensor", lambda t, fb=fb, tt=tt, bwd=bwd: t.matmul(
                        pb_y[tt][:, :], h_sb[:, fb * 512 + tt * 128:fb * 512 + tt * 128 + 128],
                        wdb[bwd][:, (fb % 8) * 512:(fb % 8 + 1) * 512],
                        start=(fb == 0), stop=False), waits, tA)
                if fb % 8 == 7 or fb == NFB - 1:
                    t_wd[bwd] = tA.count
            for ic in range(32):
                kb = ic // 4
                bwo = kb % 2
                if ic % 4 == 0:
                    dw = [(tA, t_wo[bwo])]
                    if do == 0 and kb == 0:
                        dw += [(aA, a_silu), (tA, t_a_end)]
                    DMA("scalar", lambda s, do=do, kb=kb, bwo=bwo: s.dma_start(
                        out=wob[bwo][:, :], in_=wo_view[do][:, kb * 4:(kb + 1) * 4, :]), dw, s_wu)
                    w_wo = (s_wu, s_wu.count)
                for tt in range(4):
                    waits = [w_wo] if (ic % 4 == 0 and tt == 0) else []
                    EM("tensor", lambda t, ic=ic, tt=tt, bwo=bwo: t.matmul(
                        pb_y[tt][:, :], otx[:, ic * 512 + tt * 128:ic * 512 + tt * 128 + 128],
                        wob[bwo][:, (ic % 4) * 512:(ic % 4 + 1) * 512],
                        start=False, stop=False), waits, tA)
                if ic % 4 == 3:
                    t_wo[bwo] = tA.count
            for tt in range(4):
                EM("tensor", lambda t, do=do, tt=tt: t.matmul(
                    pb_y[tt][:, :], sb_ones[:, :], sb_bop[:, do * 512:(do + 1) * 512],
                    start=False, stop=True), [], tA)
            t_bo = tA.count
            for tt in range(4):
                ob = tt % 2
                DMA("scalar", lambda s, th=th, do=do, tt=tt, ob=ob: s.dma_start(
                    out=xrb[ob][:, :],
                    in_=x_in[th * 512 + tt * 128:th * 512 + tt * 128 + 128,
                             do * 512:(do + 1) * 512]),
                    [(vA, v_orb[ob])] + ([(tA, t_a_end)] if (do == 0 and tt < 2) else []),
                    s_xr)
                EM("vector", lambda v, tt=tt, ob=ob: v.tensor_tensor(
                    orb[ob][:, :], pb_y[tt][:, :], xrb[ob][:, :], ALU.add),
                    [(tA, t_bo), (s_xr, s_xr.count), (s_out, s_out.count)], vA)
                v_yevac[tt] = vA.count
                v_orb[ob] = vA.count
                DMA("scalar", lambda s, th=th, do=do, tt=tt, ob=ob: s.dma_start(
                    out=out_dram[th * 512 + tt * 128:th * 512 + tt * 128 + 128,
                                 do * 512:(do + 1) * 512],
                    in_=orb[ob][:, :]), [(vA, vA.count)], s_out)
                s_orb[ob] = s_out.count
        t_half_end = tA.count
        v_b_end = vA.count
        s_out_b = s_out.count

    EM("scalar", None, [(s_out, s_out.count)])
    EM("sync", None, [(s_out, s_out.count)])
    EM("vector", None, [(s_out, s_out.count)])

    targets = rec.collect_targets()
    with nc.Block() as block:
        block.tensor(lambda engine: rec.replay("tensor", engine, targets))
        block.vector(lambda engine: rec.replay("vector", engine, targets))
        block.scalar(lambda engine: rec.replay("scalar", engine, targets))
        block.gpsimd(lambda engine: rec.replay("gpsimd", engine, targets))
        block.sync(lambda engine: rec.replay("sync", engine, targets))

    return nc, st



def _tile_w(w, cols):
    o, i = w.shape
    return np.ascontiguousarray(
        w.reshape(o // cols, cols, i // 128, 128).transpose(0, 2, 3, 1))


def build_masks(r):
    blks = rank_blocks(r)
    m = np.zeros((128, MASK_COLS), dtype=np.float32)
    kk = np.arange(128)[:, None]
    qq = np.arange(128)[None, :]
    tri = np.where(kk <= qq, 0.0, NEG).astype(np.float32)
    for (g, j, act, off) in SCHED:
        for idx, sl in enumerate(act):
            gq = blks[sl]
            col = off + idx * 128
            if j == gq:
                m[:, col:col + 128] = tri
            elif j > gq:
                m[:, col:col + 128] = NEG
    return m.astype(ml_dtypes.bfloat16)


_PROG_CACHE = {}


def _get_prog():
    if "p" not in _PROG_CACHE:
        _PROG_CACHE["p"] = build_program()
    return _PROG_CACHE["p"][0]


def kernel(L0_attn_x, L0_attn_Wq, L0_attn_bq, L0_attn_Wk, L0_attn_bk,
           L0_attn_Wv, L0_attn_bv, L0_attn_Wo, L0_attn_bo,
           L0_ffn_W_gate, L0_ffn_W_up, L0_ffn_W_down):
    x = np.asarray(L0_attn_x, np.float32)
    wq_t = _tile_w(np.asarray(L0_attn_Wq, np.float32), 128)
    wk_t = _tile_w(np.asarray(L0_attn_Wk, np.float32), 128)
    wv_t = _tile_w(np.asarray(L0_attn_Wv, np.float32), 512)
    wo_t = _tile_w(np.asarray(L0_attn_Wo, np.float32), 512)
    wg_t = _tile_w(np.asarray(L0_ffn_W_gate, np.float32), 128)
    wu_t = _tile_w(np.asarray(L0_ffn_W_up, np.float32), 128)
    wd_t = _tile_w(np.asarray(L0_ffn_W_down, np.float32), 512)
    wflat = np.concatenate([wg_t.ravel(), wu_t.ravel(), wd_t.ravel(), wo_t.ravel()])
    wflat = wflat.reshape(E_W // 512, 512)
    WR = E_W // 8 // 512
    bq_t = np.ascontiguousarray(np.asarray(L0_attn_bq, np.float32).reshape(32, 128).T)
    bk_t = np.ascontiguousarray(np.asarray(L0_attn_bk, np.float32).reshape(32, 128).T)
    bv_row = np.ascontiguousarray(np.asarray(L0_attn_bv, np.float32).reshape(1, D))
    bo_row = np.ascontiguousarray(np.asarray(L0_attn_bo, np.float32).reshape(1, D))
    ident = np.eye(128, dtype=ml_dtypes.bfloat16)

    nc = _get_prog()
    in_maps = []
    for c in range(8):
        b, r = c // 4, c % 4
        blks = rank_blocks(r)
        rows = np.concatenate([np.arange(j * 128, (j + 1) * 128) for j in blks])
        in_maps.append(dict(
            x_core=np.ascontiguousarray(x[b][rows]),
            wq_t=wq_t, wk_t=wk_t, wv_t=wv_t,
            wslice=np.ascontiguousarray(wflat[c * WR:(c + 1) * WR]),
            bq_t=bq_t, bk_t=bk_t, bv_row=bv_row, bo_row=bo_row,
            masks_t=build_masks(r), ident=ident,
        ))
    res = run_bass_kernel_spmd(nc, in_maps, core_ids=list(range(8)))
    out = np.empty((B, L, D), dtype=np.float32)
    for c in range(8):
        b, r = c // 4, c % 4
        blks = rank_blocks(r)
        oc = res.results[c]["out_core"]
        for sl, j in enumerate(blks):
            out[b, j * 128:(j + 1) * 128, :] = oc[sl * 128:(sl + 1) * 128, :]
    if DEBUG_OUTS:
        kernel.last_results = res
    return out


# version 11
# speedup vs baseline: 1.1656x; 1.0094x over previous
import os
os.environ.setdefault("NEURON_SCRATCHPAD_PAGE_SIZE", "512")

import numpy as np
import ml_dtypes
from contextlib import ExitStack

import concourse.bass as bass
import concourse.mybir as mybir
from concourse.bass_utils import run_bass_kernel_spmd

F32 = mybir.dt.float32
BF16 = mybir.dt.bfloat16
AF = mybir.ActivationFunctionType
ALU = mybir.AluOpType
AX = mybir.AxisListType

B, L, D, H = 2, 4096, 4096, 32
DFF, NFB = 11008, 86
T = 1024
PADS = [4, 8, 12, 16, 20, 24, 28, 32]
NEG = -1.0e9
SCALE = float(1.0 / np.sqrt(128.0))
VST = 144

SZ_WG = NFB * 32 * 128 * 128
SZ_WD = 8 * NFB * 128 * 512
SZ_WO = 8 * 32 * 128 * 512
E_W = 2 * SZ_WG + SZ_WD + SZ_WO
OFF_WU, OFF_WD, OFF_WO = SZ_WG, 2 * SZ_WG, 2 * SZ_WG + SZ_WD

DEBUG_OUTS = False


def rank_blocks(r):
    return [r, 7 - r, 8 + r, 15 - r, 16 + r, 23 - r, 24 + r, 31 - r]


def block_owner(j):
    m, u = divmod(j, 8)
    return (u, 2 * m) if u < 4 else (7 - u, 2 * m + 1)


def att_schedule():
    sched, off = [], 0
    for g in range(2):
        slots = [4 * g + i for i in range(4)]
        for j in range(PADS[slots[-1]]):
            act = [s for s in slots if PADS[s] > j]
            sched.append((g, j, act, off))
            off += 128 * len(act)
    return sched


SCHED = att_schedule()
MASK_COLS = 144 * 128


class Sem:
    def __init__(self, handle, name):
        self.h, self.name, self.count = handle, name, 0


class Rec:
    ENGINES = ("tensor", "vector", "scalar", "gpsimd", "sync")

    def __init__(self):
        self.q = {e: [] for e in self.ENGINES}

    def emit(self, eng, fn, waits=(), inc=None, inc_n=1):
        self.q[eng].append((list(waits), fn, inc, inc_n))
        if inc is not None:
            inc.count += inc_n

    def collect_targets(self):
        t = {}
        for q in self.q.values():
            for waits, _, _, _ in q:
                for s, v in waits:
                    t.setdefault(id(s), set()).add(v)
        return t

    def replay(self, name, engobj, targets):
        cum = {}
        pending = {}
        for waits, fn, inc, inc_n in self.q[name]:
            merged = {}
            for s, v in waits:
                if v > 0 and v > merged.get(id(s), (None, -1))[1]:
                    merged[id(s)] = (s, v)
            for s, v in merged.values():
                engobj.wait_ge(s.h, v)
            if fn is not None:
                inst = fn(engobj)
                if inc is not None:
                    if inc_n == 16:
                        inst.then_inc(inc.h, inc_n)
                        continue
                    c = cum.get(id(inc), 0) + inc_n
                    cum[id(inc)] = c
                    p = pending.get(id(inc), 0) + inc_n
                    if c in targets.get(id(inc), ()):
                        inst.then_inc(inc.h, p)
                        pending[id(inc)] = 0
                    else:
                        pending[id(inc)] = p


def build_program():
    nc = bass.Bass("TRN2", target_bir_lowering=False)
    dt = nc.dram_tensor

    x_in = dt("x_core", [T, D], F32, kind="ExternalInput").ap()
    wq_t = dt("wq_t", [H, 32, 128, 128], F32, kind="ExternalInput").ap()
    wk_t = dt("wk_t", [H, 32, 128, 128], F32, kind="ExternalInput").ap()
    wv_t = dt("wv_t", [8, 32, 128, 512], F32, kind="ExternalInput").ap()
    wslice = dt("wslice", [E_W // 8 // 512, 512], F32, kind="ExternalInput").ap()
    bq_in = dt("bq_t", [128, 32], F32, kind="ExternalInput").ap()
    bk_in = dt("bk_t", [128, 32], F32, kind="ExternalInput").ap()
    bv_in = dt("bv_row", [1, D], F32, kind="ExternalInput").ap()
    bo_in = dt("bo_row", [1, D], F32, kind="ExternalInput").ap()
    masks_in = dt("masks_t", [128, MASK_COLS], BF16, kind="ExternalInput").ap()
    ident_in = dt("ident", [128, 128], BF16, kind="ExternalInput").ap()
    out_dram = dt("out_core", [T, D], F32, kind="ExternalOutput").ap()
    if DEBUG_OUTS:
        dbg_xn = dt("dbg_xn", [128, 32 * T], BF16, kind="ExternalOutput").ap()
        dbg_kv = dt("dbg_kv", [4 * 16400, 512], BF16, kind="ExternalOutput").ap()
        dbg_ot = dt("dbg_ot", [D, T], BF16, kind="ExternalOutput").ap()

    KVROWS = 16400
    kv_src = dt("kv_src", [KVROWS, 512], BF16, kind="Internal")
    kv_gath = dt("kv_gath", [4 * KVROWS, 512], BF16, kind="Internal")
    ot_dram = dt("ot_dram", [D, T], BF16, kind="Internal")
    xn_spill = dt("xn_spill", [128, 32 * T], BF16, kind="Internal")

    KSZ = H * 128 * T
    kview = kv_src.ap().rearrange("a b -> (a b)")[0:KSZ] \
        .rearrange("(h p t) -> h p t", h=H, p=128)
    vview = kv_src.ap().rearrange("a b -> (a b)")[KSZ:2 * KSZ] \
        .rearrange("(t d) -> t d", t=T)
    norms_row = kv_src.ap()[16384:16385, 0:H]

    gsp = kv_gath.ap().rearrange("a b -> (a b)").rearrange("(r z) -> r z", r=4)
    gK = gsp[:, 0:KSZ].rearrange("r (h p s t) -> h p r s t", h=H, p=128, s=8)
    gV = gsp[:, KSZ:2 * KSZ].rearrange("r (s t h c) -> h t r s c", s=8, t=128, h=H)
    gN = gsp[:, 16384 * 512:16384 * 512 + H].rearrange("r h -> h r")

    ag_w_in = dt("ag_w_in", [E_W // 8 // 512, 512], BF16, kind="Internal")
    ag_w_out = dt("ag_w_out", [E_W // 512, 512], BF16, kind="Internal", addr_space="Shared")
    wff = ag_w_out.ap().rearrange("a b -> (a b)")
    wg_view = wff[0:SZ_WG].rearrange("(fb dc p f) -> fb p dc f", fb=NFB, dc=32, p=128)
    wu_view = wff[OFF_WU:OFF_WU + SZ_WG].rearrange("(fb dc p f) -> fb p dc f", fb=NFB, dc=32, p=128)
    wd_view = wff[OFF_WD:OFF_WD + SZ_WD].rearrange("(do fb p f) -> do p fb f", do=8, fb=NFB, p=128)
    wo_view = wff[OFF_WO:OFF_WO + SZ_WO].rearrange("(do ic p f) -> do p ic f", do=8, ic=32, p=128)

    st = ExitStack()
    e = st.enter_context

    def sem(name):
        return Sem(e(nc.semaphore(name)), name)

    s_x = sem("s_x")
    s_w = sem("s_w")
    s_kvw = sem("s_kvw")
    s_gr = sem("s_gr")
    s_ot = sem("s_ot")
    s_sp = sem("s_sp")
    s_ms = sem("s_ms")
    s_xr = sem("s_xr")
    s_out = sem("s_out")
    s_cc = sem("s_cc")
    s_cv = sem("s_cv")
    s_wu = sem("s_wu")
    tA, vA, aA, gA = sem("tA"), sem("vA"), sem("aA"), sem("gA")
    s_x2 = [sem(f"s_x2{i}") for i in range(2)]
    s_wk = [sem(f"s_wk{i}") for i in range(2)]
    s_wv = [sem(f"s_wv{i}") for i in range(2)]
    s_wq = [sem(f"s_wq{i}") for i in range(2)]
    s_kh = [sem(f"s_kh{i}") for i in range(2)]
    s_vh = [sem(f"s_vh{i}") for i in range(2)]
    s_nk2 = sem("s_nk2")
    s_wg2 = [sem(f"s_wg2{i}") for i in range(2)]
    s_wu2 = [sem(f"s_wu2{i}") for i in range(2)]
    s_wd2 = [sem(f"s_wd2{i}") for i in range(2)]
    s_wo2 = [sem(f"s_wo2{i}") for i in range(2)]
    s_xr2 = [sem(f"s_xr2{i}") for i in range(2)]

    rec = Rec()

    def EM(eng, fn, waits=(), inc=None, inc_n=1):
        waits = list(waits)
        if fn is not None:
            if eng == "vector":
                waits.append((vA, vA.count))
            elif eng == "scalar":
                waits.append((aA, aA.count))
        rec.emit(eng, fn, waits, inc, inc_n)

    def DMA(eng, fn, waits, s):
        rec.emit(eng, fn, waits, inc=s, inc_n=16)

    sb_ident = e(nc.sbuf_tensor("sb_ident", [128, 128], BF16))
    sb_ones = e(nc.sbuf_tensor("sb_ones", [128, 128], BF16))
    sb_bq = e(nc.sbuf_tensor("sb_bq", [128, 32], F32))
    sb_bk = e(nc.sbuf_tensor("sb_bk", [128, 32], F32))
    sb_bvp = e(nc.sbuf_tensor("sb_bvp", [128, D], BF16))
    sb_bop = e(nc.sbuf_tensor("sb_bop", [128, D], BF16))
    sb_sm = e(nc.sbuf_tensor("sb_sm", [128, 16], F32))
    sb_bc = e(nc.sbuf_tensor("sb_bc", [128, 16], BF16))
    sb_nk = e(nc.sbuf_tensor("sb_nk", [1, 128], BF16))
    (C_S1, C_S2, C_MEAN, C_VAR, C_STD, C_RSTD, C_MRS, C_M2,
     C_MQ, C_MK, C_T0, C_T1, C_INVL, C_MHB, C_U1, C_U2) = range(16)

    def smc(i):
        return sb_sm[:, i:i + 1]

    base = (nc.sbuf_base + 63) & ~63
    top = nc.sbuf_top

    def at(name, shape, dtype, off):
        nbytes = shape[1] * (2 if dtype == BF16 else 4)
        assert off + nbytes <= top, f"{name}: {off}+{nbytes} > {top}"
        return nc.alloc_sbuf_tensor_at(name, shape, dtype, offset=off)

    xn_d = at("xn_d", [128, 32 * T], BF16, base)
    SUB = base + 65536
    xt = [at(f"xt{i}", [128, D], F32, SUB + i * 16384) for i in range(2)]
    xsq = at("xsq", [128, D], F32, SUB + 32768)
    xnt = at("xnt", [128, D], BF16, SUB + 49152)
    wkb = [at(f"wkb{i}", [128, 32 * 128], BF16, SUB + i * 8192) for i in range(2)]
    kqb = [at(f"kqb{i}", [128, T], BF16, SUB + 16384 + i * 2048) for i in range(2)]
    ksq = at("ksq", [128, T], BF16, SUB + 20480)
    nacc = at("nacc", [1, 32], BF16, SUB + 22528)
    wvb = [at(f"wvb{i}", [128, 32 * 512], BF16, SUB + 23040 + i * 32768) for i in range(2)]
    vtb = [at(f"vtb{i}", [128, 512], BF16, SUB + 88576 + i * 1024) for i in range(2)]
    sb_mask = at("sb_mask", [128, MASK_COLS], BF16, SUB)
    kh = [at(f"kh{i}", [128, 32 * 128], BF16, SUB + 36864 + i * 8192) for i in range(2)]
    vh = [at(f"vh{i}", [128, 32 * VST], BF16, SUB + 53248 + i * 9216) for i in range(2)]
    wqb = [at(f"wqb{i}", [128, 32 * 128], BF16, SUB + 71680 + i * 8192) for i in range(2)]
    qh = at("qh", [128, T], BF16, SUB + 90624)
    qsq = at("qsq", [128, T], BF16, SUB + 92672)
    ptb = [at(f"ptb{i}", [128, 512], BF16, SUB + 96768 + i * 1024) for i in range(4)]
    osb = at("osb", [128, 128], BF16, SUB + 100864)
    oth = [at(f"oth{i}", [128, T], BF16, SUB + 101120 + i * 2048) for i in range(2)]
    assert SUB + 105216 <= top, SUB + 105216

    h_sb = at("h_sb", [128, NFB * 512], BF16, base)
    xnh = at("xnh", [128, 32 * 512], BF16, base + 88064)
    otx = at("otx", [128, 32 * 512], BF16, base + 120832)
    PB = base + 153600
    wgb = [at(f"wgb{i}", [128, 32 * 128], BF16, PB + i * 8192) for i in range(2)]
    wub = [at(f"wub{i}", [128, 32 * 128], BF16, PB + 16384 + i * 8192) for i in range(2)]
    gsb = [at(f"gsb{i}", [128, 512], BF16, PB + 32768 + i * 1024) for i in range(2)]
    wdb = [at(f"wdb{i}", [128, 8 * 512], BF16, PB + i * 8192) for i in range(2)]
    wob = [at(f"wob{i}", [128, 4 * 512], BF16, PB + 16384 + i * 4096) for i in range(2)]
    xrb = [at(f"xrb{i}", [128, 512], F32, PB + 24576 + i * 2048) for i in range(2)]
    orb = [at(f"orb{i}", [128, 512], F32, PB + 28672 + i * 2048) for i in range(2)]
    assert PB + 34816 <= top

    pb_y = [e(nc.psum_tensor(f"pb_y{i}", [128, 512], F32)) for i in range(4)]
    pb_s = [e(nc.psum_tensor(f"pb_s{i}", [128, 512], F32)) for i in range(2)]
    pb_t = [e(nc.psum_tensor(f"pb_t{i}", [128, 256], BF16)) for i in range(2)]

    DMA("sync", lambda s: s.dma_start(out=sb_ident[:, :], in_=ident_in), [], s_ms)
    DMA("sync", lambda s: s.dma_start(out=sb_bq[:, :], in_=bq_in), [], s_ms)
    DMA("sync", lambda s: s.dma_start(out=sb_bk[:, :], in_=bk_in), [], s_ms)
    s_const = s_ms.count
    EM("gpsimd", lambda g: g.memset(sb_ones[:, :], 1.0), [], gA)
    EM("gpsimd", lambda g: g.memset(sb_bvp[:, :], 0.0), [], gA)
    EM("gpsimd", lambda g: g.memset(sb_bop[:, :], 0.0), [], gA)
    EM("gpsimd", lambda g: g.memset(sb_bc[:, :], 0.0), [], gA)
    g_const = gA.count
    DMA("gpsimd", lambda g: g.dma_start(out=sb_bvp[0:1, :], in_=bv_in), [(gA, g_const)], s_w)
    DMA("gpsimd", lambda g: g.dma_start(out=sb_bop[0:1, :], in_=bo_in), [(gA, g_const)], s_w)
    EM("vector", lambda v: v.tensor_scalar_mul(sb_bq[:, :], sb_bq[:, :], SCALE),
       [(s_ms, s_const)], vA)
    v_bq = vA.count

    a_lnapply = 0
    t_tr_done = {0: 0, 1: 0}
    v_tr_done = {0: 0, 1: 0}
    for tt in range(8):
        bp = tt % 2
        w = [(aA, a_lnapply)] if tt >= 2 else []
        DMA("sync", lambda s, tt=tt, bp=bp: s.dma_start(
            out=xt[bp][:, :], in_=x_in[tt * 128:(tt + 1) * 128, :]), w, s_x2[bp])
        wx = [(s_x2[bp], s_x2[bp].count)]
        EM("vector", lambda v, bp=bp: v.reduce_sum(smc(C_S1), xt[bp][:, :], axis=AX.X), wx, vA)
        EM("scalar", lambda s, bp=bp: s.activation(xsq[:, :], xt[bp][:, :], AF.Square),
           wx + [(vA, vA.count)], aA)
        EM("vector", lambda v: v.reduce_sum(smc(C_S2), xsq[:, :], axis=AX.X), [(aA, aA.count)], vA)
        EM("vector", lambda v: v.tensor_scalar_mul(smc(C_MEAN), smc(C_S1), 1.0 / D), [], vA)
        EM("vector", lambda v: v.tensor_mul(smc(C_M2), smc(C_MEAN), smc(C_MEAN)), [], vA)
        EM("vector", lambda v: v.tensor_scalar(smc(C_VAR), smc(C_S2), 1.0 / D, None, ALU.mult),
           [(aA, a_lnapply)], vA)
        EM("vector", lambda v: v.tensor_sub(smc(C_VAR), smc(C_VAR), smc(C_M2)), [], vA)
        EM("vector", lambda v: v.tensor_scalar_add(smc(C_VAR), smc(C_VAR), 1e-5), [], vA)
        EM("scalar", lambda s: s.activation(smc(C_STD), smc(C_VAR), AF.Sqrt), [(vA, vA.count)], aA)
        EM("vector", lambda v: v.reciprocal(smc(C_RSTD), smc(C_STD)),
           [(aA, aA.count)], vA)
        EM("vector", lambda v: v.tensor_mul(smc(C_MRS), smc(C_MEAN), smc(C_RSTD)), [], vA)
        EM("vector", lambda v: v.tensor_scalar_mul(smc(C_MRS), smc(C_MRS), -1.0), [], vA)
        EM("scalar", lambda s, bp=bp: s.activation(
            xnt[:, :], xt[bp][:, :], AF.Identity, bias=smc(C_MRS), scale=smc(C_RSTD)),
            [(vA, vA.count), (tA, tA.count)], aA)
        a_lnapply = aA.count
        for c in range(32):
            pp = c % 2
            EM("tensor", lambda t, c=c, pp=pp: t.transpose(
                pb_t[pp][:, 0:128], xnt[:, c * 128:(c + 1) * 128], sb_ident[:, :]),
                ([(aA, a_lnapply), (s_ms, s_const)] if c == 0 else [])
                + [(vA, v_tr_done[pp])], tA)
            t_tr_done[pp] = tA.count
            EM("vector", lambda v, c=c, pp=pp, tt=tt: v.tensor_copy(
                xn_d[:, c * T + tt * 128: c * T + tt * 128 + 128], pb_t[pp][:, 0:128]),
                [(tA, t_tr_done[pp])], vA)
            v_tr_done[pp] = vA.count
    v_xn = vA.count

    if DEBUG_OUTS:
        DMA("sync", lambda s: s.dma_start(out=dbg_xn, in_=xn_d[:, :]), [(vA, v_xn)], s_ms)

    t_kproj = {}
    a_kevac = {0: 0, 1: 0}
    a_ksq_head = {}
    v_knorm = 0
    s_kvw_head = {}
    for h in range(H):
        bp = h % 2
        w = [(tA, t_kproj[h - 2])] if h >= 2 else [(vA, v_xn), (aA, a_lnapply)]
        DMA("gpsimd", lambda g, h=h, bp=bp: g.dma_start(
            out=wkb[bp][:, :], in_=wk_t[h].rearrange("a b c -> b a c")), w, s_wk[bp])
        w_w = (s_wk[bp], s_wk[bp].count)
        for th in range(2):
            for dc in range(32):
                waits = []
                if dc == 0:
                    waits = [w_w, (vA, v_xn)]
                    if h >= 1:
                        waits.append((aA, a_kevac[th]))
                EM("tensor", lambda t, th=th, dc=dc, bp=bp: t.matmul(
                    pb_y[th][:, :], wkb[bp][:, dc * 128:(dc + 1) * 128],
                    xn_d[:, dc * T + th * 512:dc * T + (th + 1) * 512],
                    start=(dc == 0), stop=(dc == 31)), waits, tA)
            waits = [(tA, tA.count)]
            if h >= 2 and th == 0:
                waits += [(s_kvw, s_kvw.count), (aA, a_ksq_head[h - 2])]
            EM("scalar", lambda s, th=th, h=h, bp=bp: s.activation(
                kqb[bp][:, th * 512:(th + 1) * 512], pb_y[th][:, :], AF.Identity,
                bias=sb_bk[:, h:h + 1]), waits, aA)
            a_kevac[th] = aA.count
        t_kproj[h] = tA.count
        a_kq = aA.count
        EM("scalar", lambda s, bp=bp: s.activation(ksq[:, :], kqb[bp][:, :], AF.Square),
           [(vA, v_knorm)], aA)
        a_ksq_head[h] = aA.count
        a_sq = aA.count
        for th in range(2):
            EM("tensor", lambda t, th=th: t.matmul(
                pb_s[th][0:1, 0:512], sb_ones[:, 0:1], ksq[:, th * 512:(th + 1) * 512],
                start=True, stop=True),
                ([(aA, a_sq)] + ([(vA, v_knorm)] if h >= 1 else [])) if th == 0 else [], tA)
            EM("vector", lambda v, th=th: v.reduce_max(
                (smc(C_T0) if th == 0 else smc(C_T1))[0:1, :], pb_s[th][0:1, 0:512], axis=AX.X),
                [(tA, tA.count)], vA)
        EM("vector", lambda v: v.tensor_max(smc(C_U1)[0:1, :], smc(C_T0)[0:1, :], smc(C_T1)[0:1, :]), [], vA)
        EM("vector", lambda v, h=h: v.tensor_copy(nacc[0:1, h:h + 1], smc(C_U1)[0:1, :]), [], vA)
        v_knorm = vA.count
        DMA("sync", lambda s, h=h, bp=bp: s.dma_start(out=kview[h], in_=kqb[bp][:, :]),
            [(aA, a_kq)], s_kvw)
        s_kvw_head[h] = s_kvw.count
    DMA("sync", lambda s: s.dma_start(out=norms_row, in_=nacc[0:1, 0:H]), [(vA, v_knorm)], s_kvw)

    t_vmm = {0: 0, 1: 0}
    v_vevac = {0: 0, 1: 0}
    s_vdma = {0: 0, 1: 0}
    for hg in range(8):
        bw = hg % 2
        w = [(tA, t_vmm[bw])] if hg >= 2 else [(vA, v_xn)]
        DMA("gpsimd", lambda g, hg=hg, bw=bw: g.dma_start(
            out=wvb[bw][:, :], in_=wv_t[hg].rearrange("a b c -> b a c")), w, s_wv[bw])
        w_w = (s_wv[bw], s_wv[bw].count)
        for tt in range(8):
            pbv = pb_s[tt % 2]
            for dc in range(32):
                waits = []
                if dc == 0:
                    waits = [w_w, (vA, v_vevac[tt % 2])]
                    if hg == 0 and tt < 2:
                        waits.append((vA, v_knorm))
                EM("tensor", lambda t, tt=tt, dc=dc, bw=bw, pbv=pbv: t.matmul(
                    pbv[:, :], xn_d[:, dc * T + tt * 128:dc * T + tt * 128 + 128],
                    wvb[bw][:, dc * 512:(dc + 1) * 512],
                    start=(dc == 0), stop=False), waits, tA)
            EM("tensor", lambda t, hg=hg, pbv=pbv: t.matmul(
                pbv[:, :], sb_ones[:, :], sb_bvp[:, hg * 512:(hg + 1) * 512],
                start=False, stop=True), [], tA)
            t_vmm[bw] = tA.count
            EM("vector", lambda v, tt=tt, pbv=pbv: v.tensor_copy(vtb[tt % 2][:, :], pbv[:, :]),
               [(tA, tA.count), (s_kvw, s_kvw.count)], vA)
            v_vevac[tt % 2] = vA.count
            DMA("sync", lambda s, hg=hg, tt=tt: s.dma_start(
                out=vview[tt * 128:(tt + 1) * 128, hg * 512:(hg + 1) * 512],
                in_=vtb[tt % 2][:, :]), [(vA, vA.count)], s_kvw)
            s_vdma[tt % 2] = s_kvw.count
    t_p1, v_p1, a_p1 = tA.count, vA.count, aA.count

    EM("gpsimd", lambda g: g.collective_compute(
        "AllGather", ALU.bypass, replica_groups=[[0, 1, 2, 3], [4, 5, 6, 7]],
        ins=[kv_src.ap()], outs=[kv_gath.ap()]), [(s_kvw, s_kvw.count)], s_cc, 1)

    if DEBUG_OUTS:
        DMA("sync", lambda s: s.dma_start(out=dbg_kv, in_=kv_gath.ap()), [(s_cc, 1)], s_ms)

    p1_bar = [(tA, t_p1), (vA, v_p1), (aA, a_p1), (s_kvw, s_kvw.count), (s_w, s_w.count),
              (s_wk[0], s_wk[0].count), (s_wk[1], s_wk[1].count),
              (s_wv[0], s_wv[0].count), (s_wv[1], s_wv[1].count)]
    DMA("sync", lambda s: s.dma_start(out=sb_mask[:, :], in_=masks_in), p1_bar, s_ms)
    s_mask_ld = s_ms.count
    for b in range(2):
        for rs in range(32):
            EM("vector", lambda v, b=b, rs=rs: v.memset(
                vh[b][:, rs * VST + 128:rs * VST + 129], 1.0),
                p1_bar if (b == 0 and rs == 0) else [], vA)
    v_vones = vA.count
    DMA("sync", lambda s: s.dma_start(out=sb_nk[0:1, :], in_=gN,
        allow_slow_non_contiguous=True), [(s_cc, 1)], s_nk2)

    t_head = {}
    v_head = {}
    a_exphead = {}
    s_ot_head = {}
    v_stevac = {0: 0, 1: 0}
    a_stexp = {0: 0, 1: 0}
    v_oaug = {0: 0, 1: 0, 2: 0, 3: 0}
    v_otcopy = {0: 0, 1: 0}
    t_osb = 0
    v_mhb_prev = 0

    for h in range(H):
        bp = h % 2
        w = [(s_cc, 1)]
        if h >= 2:
            w += [(tA, t_head[h - 2])]
        for r4 in range(4):
            DMA("sync", lambda s, h=h, bp=bp, r4=r4: s.dma_start(
                out=kh[bp][:, r4 * 1024:(r4 + 1) * 1024], in_=gK[h][:, r4]),
                w if r4 == 0 else [], s_kh[bp])
        for r4 in range(4):
            DMA("sync", lambda s, h=h, bp=bp, r4=r4: s.dma_start(
                out=vh[bp].ap().rearrange("p (s c) -> p s c", c=VST)[:, r4 * 8:(r4 + 1) * 8, 0:128],
                in_=gV[h][:, r4]),
                [(vA, v_vones)] if (h < 2 and r4 == 0) else [], s_vh[bp])
        w_kv = (s_kh[bp], s_kh[bp].count)
        w_vv = (s_vh[bp], s_vh[bp].count)
        wv2 = [(tA, t_head[h - 2])] if h >= 2 else list(p1_bar)
        DMA("gpsimd", lambda g, h=h, bp=bp: g.dma_start(
            out=wqb[bp][:, :], in_=wq_t[h].rearrange("a b c -> b a c")), wv2, s_wq[bp])
        w_wq = (s_wq[bp], s_wq[bp].count)
        if h < 4:
            CROWS = E_W // 8 // 512 // 4
            DMA("gpsimd", lambda g, h=h: g.dma_start(
                out=ag_w_in.ap()[h * CROWS:(h + 1) * CROWS, :],
                in_=wslice[h * CROWS:(h + 1) * CROWS, :]), [], s_cv)
        if h == 3:
            EM("gpsimd", lambda g: g.collective_compute(
                "AllGather", ALU.bypass, replica_groups=[list(range(8))],
                ins=[ag_w_in.ap()], outs=[ag_w_out.ap()]),
                [(s_cv, s_cv.count)], s_cc, 1)

        for th in range(2):
            for dc in range(32):
                waits = []
                if dc == 0:
                    waits = [w_wq, (vA, v_xn)]
                    waits.append((aA, a_stexp[th]) if h >= 1 else (vA, v_p1))
                    if h == 0:
                        waits.append((vA, v_bq))
                EM("tensor", lambda t, th=th, dc=dc, bp=bp: t.matmul(
                    pb_s[th][:, :], wqb[bp][:, dc * 128:(dc + 1) * 128],
                    xn_d[:, dc * T + th * 512:dc * T + (th + 1) * 512],
                    start=(dc == 0), stop=(dc == 31)), waits, tA)
            waits = [(tA, tA.count)]
            if h >= 1 and th == 0:
                waits.append((aA, a_exphead[h - 1]))
            EM("scalar", lambda s, th=th, h=h: s.activation(
                qh[:, th * 512:(th + 1) * 512], pb_s[th][:, :], AF.Identity,
                bias=sb_bq[:, h:h + 1], scale=SCALE), waits, aA)
        EM("scalar", lambda s: s.activation(qsq[:, :], qh[:, :], AF.Square), [], aA)
        a_qsq = aA.count
        for th in range(2):
            EM("tensor", lambda t, th=th: t.matmul(
                pb_s[th][0:1, 0:512], sb_ones[:, 0:1], qsq[:, th * 512:(th + 1) * 512],
                start=True, stop=True),
                [(aA, a_qsq)] if th == 0 else [], tA)
            EM("vector", lambda v, th=th: v.reduce_max(
                (smc(C_T0) if th == 0 else smc(C_T1))[0:1, :], pb_s[th][0:1, 0:512], axis=AX.X),
                [(tA, tA.count)], vA)
        EM("vector", lambda v: v.tensor_max(smc(C_MQ)[0:1, :], smc(C_T0)[0:1, :], smc(C_T1)[0:1, :]), [], vA)
        EM("vector", lambda v, h=h: v.reduce_max(
            smc(C_MK)[0:1, :], sb_nk[0:1, h * 4:(h + 1) * 4], axis=AX.X),
            [(s_nk2, 16)], vA)
        EM("vector", lambda v: v.tensor_mul(smc(C_T0)[0:1, :], smc(C_MQ)[0:1, :], smc(C_MK)[0:1, :]), [], vA)
        EM("scalar", lambda s: s.activation(smc(C_T1)[0:1, :], smc(C_T0)[0:1, :], AF.Ln), [(vA, vA.count)], aA)
        EM("scalar", lambda s: s.activation(smc(C_T0)[0:1, :], smc(C_T1)[0:1, :], AF.Exp, scale=0.5), [], aA)
        EM("vector", lambda v: v.tensor_scalar_mul(smc(C_U1)[0:1, :], smc(C_T0)[0:1, :], -1.0),
           [(aA, aA.count)] + ([(aA, a_exphead[h - 1])] if h >= 1 else []), vA)
        EM("vector", lambda v: v.tensor_copy(sb_bc[0:1, 0:1], smc(C_U1)[0:1, :]), [], vA)
        EM("tensor", lambda t: t.matmul(pb_s[0][:, 0:1], sb_ones[:, :], sb_bc[:, 0:1],
                                        start=True, stop=True), [(vA, vA.count)], tA)
        EM("vector", lambda v: v.tensor_copy(smc(C_MHB), pb_s[0][:, 0:1]), [(tA, tA.count)], vA)
        v_mhb = vA.count
        v_mhb_prev = v_mhb

        first_fin = True
        a_exp_si = {}

        def emit_st(si):
            g, j, act, moff = SCHED[si]
            jp = si % 2
            rp = si % 4
            n = len(act)
            qlo, qhi = act[0] * 128, (act[-1] + 1) * 128
            r_, s_ = block_owner(j)
            pos = r_ * 8 + s_
            waits = [(aA, a_stexp[jp])]
            if si == 0:
                waits += [w_kv, w_vv, (s_ms, s_mask_ld), (vA, v_mhb)]
            EM("tensor", lambda t, jp=jp, pos=pos, qlo=qlo, qhi=qhi, bp=bp: t.matmul(
                pb_s[jp][:, 0:qhi - qlo], kh[bp][:, pos * 128:(pos + 1) * 128],
                qh[:, qlo:qhi], start=True, stop=False), waits, tA)
            EM("tensor", lambda t, jp=jp, n=n, moff=moff: t.matmul(
                pb_s[jp][:, 0:n * 128], sb_ident[:, :],
                sb_mask[:, moff:moff + n * 128], start=False, stop=True), [], tA)
            t_st = tA.count
            EM("scalar", lambda s, jp=jp, rp=rp, n=n: s.activation(
                ptb[rp][:, 0:n * 128], pb_s[jp][:, 0:n * 128], AF.Exp, bias=smc(C_MHB)),
                [(tA, t_st)], aA)
            a_stexp[jp] = aA.count
            a_exp_si[si] = aA.count
            return si

        def emit_av(si):
            nonlocal first_fin, t_osb
            g, j, act, moff = SCHED[si]
            jp = si % 2
            rp = si % 4
            r_, s_ = block_owner(j)
            pos = r_ * 8 + s_
            for idx, sl in enumerate(act):
                c_in = sl % 4
                first, last = (j == 0), (j == PADS[sl] - 1)
                avw = [(aA, a_exp_si[si])] if idx == 0 else []
                if first:
                    avw.append((vA, v_oaug[c_in]))
                EM("tensor", lambda t, c_in=c_in, idx=idx, rp=rp, pos=pos, bp=bp,
                   first=first, last=last: t.matmul(
                    pb_y[c_in][:, 0:129], ptb[rp][:, idx * 128:(idx + 1) * 128],
                    vh[bp][:, pos * VST:pos * VST + 129],
                    start=first, stop=last), avw, tA)
                if last:
                    tb = sl % 2
                    EM("vector", lambda v, c_in=c_in: v.reciprocal(
                        smc(C_INVL), pb_y[c_in][:, 128:129]), [(tA, tA.count)], vA)
                    EM("vector", lambda v, c_in=c_in: v.tensor_scalar(
                        osb[:, :], pb_y[c_in][:, 0:128], smc(C_INVL), None, ALU.mult),
                        [(tA, t_osb)], vA)
                    v_oaug[c_in] = vA.count
                    EM("tensor", lambda t, tb=tb: t.transpose(
                        pb_t[tb][:, 0:128], osb[:, :], sb_ident[:, :]),
                        [(vA, v_oaug[c_in]), (vA, v_otcopy[tb])], tA)
                    t_osb = tA.count
                    cw = [(tA, t_osb)]
                    if first_fin and h >= 2:
                        cw.append((s_ot, s_ot.count))
                    first_fin = False
                    EM("vector", lambda v, sl=sl, bp=bp, tb=tb: v.tensor_copy(
                        oth[bp][:, sl * 128:(sl + 1) * 128], pb_t[tb][:, 0:128]), cw, vA)
                    v_otcopy[tb] = vA.count

        pend2 = []
        for si in range(len(SCHED)):
            emit_st(si)
            if len(pend2) == 2:
                emit_av(pend2.pop(0))
            pend2.append(si)
        for si in pend2:
            emit_av(si)
        a_exphead[h] = aA.count
        DMA("sync", lambda s, h=h, bp=bp: s.dma_start(
            out=ot_dram.ap()[h * 128:(h + 1) * 128, :], in_=oth[bp][:, :]),
            [(vA, vA.count)], s_ot)
        s_ot_head[h] = s_ot.count
        t_head[h] = tA.count
        v_head[h] = vA.count

    DMA("sync", lambda s: s.dma_start(out=xn_spill.ap(), in_=xn_d[:, :]),
        [(tA, tA.count)], s_sp)
    s_spill = s_sp.count
    if DEBUG_OUTS:
        DMA("sync", lambda s: s.dma_start(out=dbg_ot, in_=ot_dram.ap()),
            [(s_ot, s_ot.count)], s_ms)

    t_p3, v_p3, a_p3 = tA.count, vA.count, aA.count
    s_gr_p3, s_ot_p3, s_w_p3 = s_gr.count, s_ot.count, s_w.count

    BAR = [(tA, t_p3), (vA, v_p3), (aA, a_p3), (s_sp, s_spill),
           (s_ot, s_ot_p3), (s_ms, s_ms.count), (s_cc, 2), (s_cv, s_cv.count),
           (s_kh[0], s_kh[0].count), (s_kh[1], s_kh[1].count),
           (s_vh[0], s_vh[0].count), (s_vh[1], s_vh[1].count),
           (s_wq[0], s_wq[0].count), (s_wq[1], s_wq[1].count), (s_nk2, 16)]

    t_half_end = 0
    t_gate = {0: 0, 1: 0}
    a_silu = 0
    v_hmul = 0
    t_wd = {0: 0, 1: 0}
    t_wo = {0: 0, 1: 0}
    v_yevac = {0: 0, 1: 0, 2: 0, 3: 0}
    s_orb = {0: 0, 1: 0}
    v_orb = {0: 0, 1: 0}

    v_b_end = 0
    s_out_b = 0
    for th in range(2):
        hb = list(BAR) if th == 0 else [(tA, t_half_end), (vA, v_b_end), (s_out, s_out_b)]
        DMA("sync", lambda s, th=th: s.dma_start(
            out=xnh[:, :],
            in_=xn_spill.ap().rearrange("p (d t) -> p d t", d=32)[:, :, th * 512:(th + 1) * 512]),
            hb + [(s_sp, s_spill)], s_sp)
        w_xn = (s_sp, s_sp.count)
        DMA("sync", lambda s, th=th: s.dma_start(
            out=otx[:, :],
            in_=ot_dram.ap().rearrange("(ic p) t -> p ic t", p=128)[:, :, th * 512:(th + 1) * 512]),
            hb, s_ot)
        w_ot = (s_ot, s_ot.count)

        for fb in range(NFB):
            bpf = fb % 2
            if fb < 2:
                w = list(hb)
            else:
                w = [(tA, t_gate[bpf])]
            DMA("sync", lambda s, fb=fb, bpf=bpf: s.dma_start(
                out=wgb[bpf][:, :], in_=wg_view[fb]), w, s_wg2[bpf])
            DMA("scalar", lambda s, fb=fb, bpf=bpf: s.dma_start(
                out=wub[bpf][:, :], in_=wu_view[fb]), list(w), s_wu2[bpf])
            w_wf = (s_wg2[bpf], s_wg2[bpf].count)
            w_wf2 = (s_wu2[bpf], s_wu2[bpf].count)
            for dc in range(32):
                waits = [w_wf, w_wf2, w_xn, (aA, a_silu)] if dc == 0 else []
                EM("tensor", lambda t, dc=dc, bpf=bpf: t.matmul(
                    pb_s[0][:, :], wgb[bpf][:, dc * 128:(dc + 1) * 128],
                    xnh[:, dc * 512:(dc + 1) * 512],
                    start=(dc == 0), stop=(dc == 31)), waits, tA)
            t_g = tA.count
            for dc in range(32):
                waits = [(vA, v_hmul)] if dc == 0 else []
                EM("tensor", lambda t, dc=dc, bpf=bpf: t.matmul(
                    pb_s[1][:, :], wub[bpf][:, dc * 128:(dc + 1) * 128],
                    xnh[:, dc * 512:(dc + 1) * 512],
                    start=(dc == 0), stop=(dc == 31)), waits, tA)
            t_gate[bpf] = tA.count
            EM("scalar", lambda s, bpf=bpf: s.activation(
                gsb[bpf][:, :], pb_s[0][:, :], AF.Silu),
                [(tA, t_g), (vA, v_hmul)], aA)
            a_silu = aA.count
            EM("vector", lambda v, fb=fb, bpf=bpf: v.tensor_tensor(
                h_sb[:, fb * 512:(fb + 1) * 512], pb_s[1][:, :], gsb[bpf][:, :], ALU.mult),
                [(tA, t_gate[bpf]), (aA, a_silu)], vA)
            v_hmul = vA.count
        v_h_done = vA.count
        t_a_end = tA.count

        for do in range(8):
            w_wd = None
            for fb in range(NFB):
                kb = fb // 8
                bwd = kb % 2
                if fb % 8 == 0:
                    lo, hi = kb * 8, min(NFB, kb * 8 + 8)
                    dw = [(tA, t_wd[bwd])]
                    if do == 0 and kb == 0:
                        dw += [w_ot, (vA, v_h_done), (aA, a_silu), (tA, t_a_end)]
                    DMA("sync", lambda s, do=do, lo=lo, hi=hi, bwd=bwd: s.dma_start(
                        out=wdb[bwd][:, 0:(hi - lo) * 512],
                        in_=wd_view[do][:, lo:hi, :]), dw, s_wd2[bwd])
                    w_wd = (s_wd2[bwd], s_wd2[bwd].count)
                for tt in range(4):
                    waits = []
                    if fb % 8 == 0 and tt == 0:
                        waits = [w_wd, (vA, v_h_done)]
                        if fb == 0:
                            waits.append((vA, v_yevac[3]))
                    EM("tensor", lambda t, fb=fb, tt=tt, bwd=bwd: t.matmul(
                        pb_y[tt][:, :], h_sb[:, fb * 512 + tt * 128:fb * 512 + tt * 128 + 128],
                        wdb[bwd][:, (fb % 8) * 512:(fb % 8 + 1) * 512],
                        start=(fb == 0), stop=False), waits, tA)
                if fb % 8 == 7 or fb == NFB - 1:
                    t_wd[bwd] = tA.count
            for ic in range(32):
                kb = ic // 4
                bwo = kb % 2
                if ic % 4 == 0:
                    dw = [(tA, t_wo[bwo])]
                    if do == 0 and kb == 0:
                        dw += [(aA, a_silu), (tA, t_a_end)]
                    DMA("scalar", lambda s, do=do, kb=kb, bwo=bwo: s.dma_start(
                        out=wob[bwo][:, :], in_=wo_view[do][:, kb * 4:(kb + 1) * 4, :]), dw, s_wo2[bwo])
                    w_wo = (s_wo2[bwo], s_wo2[bwo].count)
                for tt in range(4):
                    waits = [w_wo] if (ic % 4 == 0 and tt == 0) else []
                    EM("tensor", lambda t, ic=ic, tt=tt, bwo=bwo: t.matmul(
                        pb_y[tt][:, :], otx[:, ic * 512 + tt * 128:ic * 512 + tt * 128 + 128],
                        wob[bwo][:, (ic % 4) * 512:(ic % 4 + 1) * 512],
                        start=False, stop=False), waits, tA)
                if ic % 4 == 3:
                    t_wo[bwo] = tA.count
            for tt in range(4):
                EM("tensor", lambda t, do=do, tt=tt: t.matmul(
                    pb_y[tt][:, :], sb_ones[:, :], sb_bop[:, do * 512:(do + 1) * 512],
                    start=False, stop=True), [], tA)
            t_bo = tA.count
            for tt in range(4):
                ob = tt % 2
                DMA("scalar", lambda s, th=th, do=do, tt=tt, ob=ob: s.dma_start(
                    out=xrb[ob][:, :],
                    in_=x_in[th * 512 + tt * 128:th * 512 + tt * 128 + 128,
                             do * 512:(do + 1) * 512]),
                    [(vA, v_orb[ob])] + ([(tA, t_a_end)] if (do == 0 and tt < 2) else []),
                    s_xr2[ob])
                EM("vector", lambda v, tt=tt, ob=ob: v.tensor_tensor(
                    orb[ob][:, :], pb_y[tt][:, :], xrb[ob][:, :], ALU.add),
                    [(tA, t_bo), (s_xr2[ob], s_xr2[ob].count), (s_out, s_out.count)], vA)
                v_yevac[tt] = vA.count
                v_orb[ob] = vA.count
                DMA("scalar", lambda s, th=th, do=do, tt=tt, ob=ob: s.dma_start(
                    out=out_dram[th * 512 + tt * 128:th * 512 + tt * 128 + 128,
                                 do * 512:(do + 1) * 512],
                    in_=orb[ob][:, :]), [(vA, vA.count)], s_out)
                s_orb[ob] = s_out.count
        t_half_end = tA.count
        v_b_end = vA.count
        s_out_b = s_out.count

    EM("scalar", None, [(s_out, s_out.count)])
    EM("sync", None, [(s_out, s_out.count)])
    EM("vector", None, [(s_out, s_out.count)])

    targets = rec.collect_targets()
    with nc.Block() as block:
        block.tensor(lambda engine: rec.replay("tensor", engine, targets))
        block.vector(lambda engine: rec.replay("vector", engine, targets))
        block.scalar(lambda engine: rec.replay("scalar", engine, targets))
        block.gpsimd(lambda engine: rec.replay("gpsimd", engine, targets))
        block.sync(lambda engine: rec.replay("sync", engine, targets))

    return nc, st



def _tile_w(w, cols):
    o, i = w.shape
    return np.ascontiguousarray(
        w.reshape(o // cols, cols, i // 128, 128).transpose(0, 2, 3, 1))


def build_masks(r):
    blks = rank_blocks(r)
    m = np.zeros((128, MASK_COLS), dtype=np.float32)
    kk = np.arange(128)[:, None]
    qq = np.arange(128)[None, :]
    tri = np.where(kk <= qq, 0.0, NEG).astype(np.float32)
    for (g, j, act, off) in SCHED:
        for idx, sl in enumerate(act):
            gq = blks[sl]
            col = off + idx * 128
            if j == gq:
                m[:, col:col + 128] = tri
            elif j > gq:
                m[:, col:col + 128] = NEG
    return m.astype(ml_dtypes.bfloat16)


_PROG_CACHE = {}


def _get_prog():
    if "p" not in _PROG_CACHE:
        _PROG_CACHE["p"] = build_program()
    return _PROG_CACHE["p"][0]


def kernel(L0_attn_x, L0_attn_Wq, L0_attn_bq, L0_attn_Wk, L0_attn_bk,
           L0_attn_Wv, L0_attn_bv, L0_attn_Wo, L0_attn_bo,
           L0_ffn_W_gate, L0_ffn_W_up, L0_ffn_W_down):
    x = np.asarray(L0_attn_x, np.float32)
    wq_t = _tile_w(np.asarray(L0_attn_Wq, np.float32), 128)
    wk_t = _tile_w(np.asarray(L0_attn_Wk, np.float32), 128)
    wv_t = _tile_w(np.asarray(L0_attn_Wv, np.float32), 512)
    wo_t = _tile_w(np.asarray(L0_attn_Wo, np.float32), 512)
    wg_t = _tile_w(np.asarray(L0_ffn_W_gate, np.float32), 128)
    wu_t = _tile_w(np.asarray(L0_ffn_W_up, np.float32), 128)
    wd_t = _tile_w(np.asarray(L0_ffn_W_down, np.float32), 512)
    wflat = np.concatenate([wg_t.ravel(), wu_t.ravel(), wd_t.ravel(), wo_t.ravel()])
    wflat = wflat.reshape(E_W // 512, 512)
    WR = E_W // 8 // 512
    bq_t = np.ascontiguousarray(np.asarray(L0_attn_bq, np.float32).reshape(32, 128).T)
    bk_t = np.ascontiguousarray(np.asarray(L0_attn_bk, np.float32).reshape(32, 128).T)
    bv_row = np.ascontiguousarray(np.asarray(L0_attn_bv, np.float32).reshape(1, D))
    bo_row = np.ascontiguousarray(np.asarray(L0_attn_bo, np.float32).reshape(1, D))
    ident = np.eye(128, dtype=ml_dtypes.bfloat16)

    nc = _get_prog()
    in_maps = []
    for c in range(8):
        b, r = c // 4, c % 4
        blks = rank_blocks(r)
        rows = np.concatenate([np.arange(j * 128, (j + 1) * 128) for j in blks])
        in_maps.append(dict(
            x_core=np.ascontiguousarray(x[b][rows]),
            wq_t=wq_t, wk_t=wk_t, wv_t=wv_t,
            wslice=np.ascontiguousarray(wflat[c * WR:(c + 1) * WR]),
            bq_t=bq_t, bk_t=bk_t, bv_row=bv_row, bo_row=bo_row,
            masks_t=build_masks(r), ident=ident,
        ))
    res = run_bass_kernel_spmd(nc, in_maps, core_ids=list(range(8)))
    out = np.empty((B, L, D), dtype=np.float32)
    for c in range(8):
        b, r = c // 4, c % 4
        blks = rank_blocks(r)
        oc = res.results[c]["out_core"]
        for sl, j in enumerate(blks):
            out[b, j * 128:(j + 1) * 128, :] = oc[sl * 128:(sl + 1) * 128, :]
    if DEBUG_OUTS:
        kernel.last_results = res
    return out


# version 12 (final)
# speedup vs baseline: 1.1700x; 1.0038x over previous
import os
os.environ.setdefault("NEURON_SCRATCHPAD_PAGE_SIZE", "512")

import numpy as np
import ml_dtypes
from contextlib import ExitStack

import concourse.bass as bass
import concourse.mybir as mybir
from concourse.bass_utils import run_bass_kernel_spmd

F32 = mybir.dt.float32
BF16 = mybir.dt.bfloat16
AF = mybir.ActivationFunctionType
ALU = mybir.AluOpType
AX = mybir.AxisListType

B, L, D, H = 2, 4096, 4096, 32
DFF, NFB = 11008, 86
T = 1024
PADS = [4, 8, 12, 16, 20, 24, 28, 32]
NEG = -1.0e9
SCALE = float(1.0 / np.sqrt(128.0))
VST = 144

SZ_WG = NFB * 32 * 128 * 128
SZ_WD = 8 * NFB * 128 * 512
SZ_WO = 8 * 32 * 128 * 512
E_W = 2 * SZ_WG + SZ_WD + SZ_WO
OFF_WU, OFF_WD, OFF_WO = SZ_WG, 2 * SZ_WG, 2 * SZ_WG + SZ_WD

DEBUG_OUTS = False


def rank_blocks(r):
    return [r, 7 - r, 8 + r, 15 - r, 16 + r, 23 - r, 24 + r, 31 - r]


def block_owner(j):
    m, u = divmod(j, 8)
    return (u, 2 * m) if u < 4 else (7 - u, 2 * m + 1)


def att_schedule():
    sched, off = [], 0
    for g in range(2):
        slots = [4 * g + i for i in range(4)]
        for j in range(PADS[slots[-1]]):
            act = [s for s in slots if PADS[s] > j]
            sched.append((g, j, act, off))
            off += 128 * len(act)
    return sched


SCHED = att_schedule()
MASK_COLS = 144 * 128


class Sem:
    def __init__(self, handle, name):
        self.h, self.name, self.count = handle, name, 0


class Rec:
    ENGINES = ("tensor", "vector", "scalar", "gpsimd", "sync")

    def __init__(self):
        self.q = {e: [] for e in self.ENGINES}

    def emit(self, eng, fn, waits=(), inc=None, inc_n=1):
        self.q[eng].append((list(waits), fn, inc, inc_n))
        if inc is not None:
            inc.count += inc_n

    def collect_targets(self):
        t = {}
        for q in self.q.values():
            for waits, _, _, _ in q:
                for s, v in waits:
                    t.setdefault(id(s), set()).add(v)
        return t

    def replay(self, name, engobj, targets):
        cum = {}
        pending = {}
        for waits, fn, inc, inc_n in self.q[name]:
            merged = {}
            for s, v in waits:
                if v > 0 and v > merged.get(id(s), (None, -1))[1]:
                    merged[id(s)] = (s, v)
            for s, v in merged.values():
                engobj.wait_ge(s.h, v)
            if fn is not None:
                inst = fn(engobj)
                if inc is not None:
                    if inc_n == 16:
                        inst.then_inc(inc.h, inc_n)
                        continue
                    c = cum.get(id(inc), 0) + inc_n
                    cum[id(inc)] = c
                    p = pending.get(id(inc), 0) + inc_n
                    if c in targets.get(id(inc), ()):
                        inst.then_inc(inc.h, p)
                        pending[id(inc)] = 0
                    else:
                        pending[id(inc)] = p


def build_program():
    nc = bass.Bass("TRN2", target_bir_lowering=False)
    dt = nc.dram_tensor

    x_in = dt("x_core", [T, D], F32, kind="ExternalInput").ap()
    wq_t = dt("wq_t", [H, 32, 128, 128], F32, kind="ExternalInput").ap()
    wk_t = dt("wk_t", [H, 32, 128, 128], F32, kind="ExternalInput").ap()
    wv_t = dt("wv_t", [8, 32, 128, 512], F32, kind="ExternalInput").ap()
    wslice = dt("wslice", [E_W // 8 // 512, 512], F32, kind="ExternalInput").ap()
    bq_in = dt("bq_t", [128, 32], F32, kind="ExternalInput").ap()
    bk_in = dt("bk_t", [128, 32], F32, kind="ExternalInput").ap()
    bv_in = dt("bv_row", [1, D], F32, kind="ExternalInput").ap()
    bo_in = dt("bo_row", [1, D], F32, kind="ExternalInput").ap()
    masks_in = dt("masks_t", [128, MASK_COLS], BF16, kind="ExternalInput").ap()
    ident_in = dt("ident", [128, 128], BF16, kind="ExternalInput").ap()
    out_dram = dt("out_core", [T, D], F32, kind="ExternalOutput").ap()
    if DEBUG_OUTS:
        dbg_xn = dt("dbg_xn", [128, 32 * T], BF16, kind="ExternalOutput").ap()
        dbg_kv = dt("dbg_kv", [4 * 16400, 512], BF16, kind="ExternalOutput").ap()
        dbg_ot = dt("dbg_ot", [D, T], BF16, kind="ExternalOutput").ap()

    KVROWS = 16400
    kv_src = dt("kv_src", [KVROWS, 512], BF16, kind="Internal")
    kv_gath = dt("kv_gath", [4 * KVROWS, 512], BF16, kind="Internal")
    ot_dram = dt("ot_dram", [D, T], BF16, kind="Internal")
    xn_spill = dt("xn_spill", [128, 32 * T], BF16, kind="Internal")

    KSZ = H * 128 * T
    kview = kv_src.ap().rearrange("a b -> (a b)")[0:KSZ] \
        .rearrange("(h p t) -> h p t", h=H, p=128)
    vview = kv_src.ap().rearrange("a b -> (a b)")[KSZ:2 * KSZ] \
        .rearrange("(t d) -> t d", t=T)
    norms_row = kv_src.ap()[16384:16385, 0:H]

    gsp = kv_gath.ap().rearrange("a b -> (a b)").rearrange("(r z) -> r z", r=4)
    gK = gsp[:, 0:KSZ].rearrange("r (h p s t) -> h p r s t", h=H, p=128, s=8)
    gV = gsp[:, KSZ:2 * KSZ].rearrange("r (s t h c) -> h t r s c", s=8, t=128, h=H)
    gN = gsp[:, 16384 * 512:16384 * 512 + H].rearrange("r h -> h r")

    ag_w_in = dt("ag_w_in", [E_W // 8 // 512, 512], BF16, kind="Internal")
    ag_w_out = dt("ag_w_out", [E_W // 512, 512], BF16, kind="Internal", addr_space="Shared")
    wff = ag_w_out.ap().rearrange("a b -> (a b)")
    wg_view = wff[0:SZ_WG].rearrange("(fb dc p f) -> fb p dc f", fb=NFB, dc=32, p=128)
    wu_view = wff[OFF_WU:OFF_WU + SZ_WG].rearrange("(fb dc p f) -> fb p dc f", fb=NFB, dc=32, p=128)
    wd_view = wff[OFF_WD:OFF_WD + SZ_WD].rearrange("(do fb p f) -> do p fb f", do=8, fb=NFB, p=128)
    wo_view = wff[OFF_WO:OFF_WO + SZ_WO].rearrange("(do ic p f) -> do p ic f", do=8, ic=32, p=128)

    st = ExitStack()
    e = st.enter_context

    def sem(name):
        return Sem(e(nc.semaphore(name)), name)

    s_x = sem("s_x")
    s_w = sem("s_w")
    s_kvw = sem("s_kvw")
    s_gr = sem("s_gr")
    s_ot = sem("s_ot")
    s_sp = sem("s_sp")
    s_ms = sem("s_ms")
    s_xr = sem("s_xr")
    s_out = sem("s_out")
    s_cc = sem("s_cc")
    s_cv = sem("s_cv")
    s_wu = sem("s_wu")
    tA, vA, aA, gA = sem("tA"), sem("vA"), sem("aA"), sem("gA")
    s_x2 = [sem(f"s_x2{i}") for i in range(2)]
    s_wk = [sem(f"s_wk{i}") for i in range(2)]
    s_wv = [sem(f"s_wv{i}") for i in range(2)]
    s_wq = [sem(f"s_wq{i}") for i in range(2)]
    s_kh = [sem(f"s_kh{i}") for i in range(2)]
    s_vh = [sem(f"s_vh{i}") for i in range(2)]
    s_nk2 = sem("s_nk2")
    s_pz = sem("s_pz")
    s_wg2 = [sem(f"s_wg2{i}") for i in range(2)]
    s_wu2 = [sem(f"s_wu2{i}") for i in range(2)]
    s_wd2 = [sem(f"s_wd2{i}") for i in range(2)]
    s_wo2 = [sem(f"s_wo2{i}") for i in range(2)]
    s_xr2 = [sem(f"s_xr2{i}") for i in range(2)]

    rec = Rec()

    def EM(eng, fn, waits=(), inc=None, inc_n=1):
        waits = list(waits)
        if fn is not None:
            if eng == "vector":
                waits.append((vA, vA.count))
            elif eng == "scalar":
                waits.append((aA, aA.count))
        rec.emit(eng, fn, waits, inc, inc_n)

    def DMA(eng, fn, waits, s):
        rec.emit(eng, fn, waits, inc=s, inc_n=16)

    sb_ident = e(nc.sbuf_tensor("sb_ident", [128, 128], BF16))
    sb_ones = e(nc.sbuf_tensor("sb_ones", [128, 128], BF16))
    sb_bq = e(nc.sbuf_tensor("sb_bq", [128, 32], F32))
    sb_bk = e(nc.sbuf_tensor("sb_bk", [128, 32], F32))
    sb_bvp = e(nc.sbuf_tensor("sb_bvp", [128, D], BF16))
    sb_bop = e(nc.sbuf_tensor("sb_bop", [128, D], BF16))
    sb_sm = e(nc.sbuf_tensor("sb_sm", [128, 16], F32))
    sb_bc = e(nc.sbuf_tensor("sb_bc", [128, 16], BF16))
    sb_nk = e(nc.sbuf_tensor("sb_nk", [1, 128], BF16))
    (C_S1, C_S2, C_MEAN, C_VAR, C_STD, C_RSTD, C_MRS, C_M2,
     C_MQ, C_MK, C_T0, C_T1, C_INVL, C_MHB, C_U1, C_U2) = range(16)

    def smc(i):
        return sb_sm[:, i:i + 1]

    base = (nc.sbuf_base + 63) & ~63
    top = nc.sbuf_top

    def at(name, shape, dtype, off):
        nbytes = shape[1] * (2 if dtype == BF16 else 4)
        assert off + nbytes <= top, f"{name}: {off}+{nbytes} > {top}"
        return nc.alloc_sbuf_tensor_at(name, shape, dtype, offset=off)

    xn_d = at("xn_d", [128, 32 * T], BF16, base)
    SUB = base + 65536
    xt = [at(f"xt{i}", [128, D], F32, SUB + i * 16384) for i in range(2)]
    xsq = at("xsq", [128, D], F32, SUB + 32768)
    xnt = at("xnt", [128, D], BF16, SUB + 49152)
    wkb = [at(f"wkb{i}", [128, 32 * 128], BF16, SUB + i * 8192) for i in range(2)]
    kqb = [at(f"kqb{i}", [128, T], BF16, SUB + 16384 + i * 2048) for i in range(2)]
    ksq = at("ksq", [128, T], BF16, SUB + 20480)
    nacc = at("nacc", [1, 32], BF16, SUB + 22528)
    wvb = [at(f"wvb{i}", [128, 32 * 512], BF16, SUB + 23040 + i * 32768) for i in range(2)]
    vtb = [at(f"vtb{i}", [128, 512], BF16, SUB + 88576 + i * 1024) for i in range(2)]
    sb_mask = at("sb_mask", [128, MASK_COLS], BF16, SUB)
    kh = [at(f"kh{i}", [128, 32 * 128], BF16, SUB + 36864 + i * 8192) for i in range(2)]
    vh = [at(f"vh{i}", [128, 32 * VST], BF16, SUB + 53248 + i * 9216) for i in range(2)]
    wqb = [at(f"wqb{i}", [128, 32 * 128], BF16, SUB + 71680 + i * 8192) for i in range(2)]
    qh = at("qh", [128, T], BF16, SUB + 90624)
    qsq = at("qsq", [128, T], BF16, SUB + 92672)
    ptb = [at(f"ptb{i}", [128, 512], BF16, SUB + 96768 + i * 1024) for i in range(4)]
    osb = at("osb", [128, 128], BF16, SUB + 100864)
    oth = [at(f"oth{i}", [128, T], BF16, SUB + 101120 + i * 2048) for i in range(2)]
    assert SUB + 105216 <= top, SUB + 105216

    h_sb = at("h_sb", [128, NFB * 512], BF16, base)
    xnh = at("xnh", [128, 32 * 512], BF16, base + 88064)
    otx = at("otx", [128, 32 * 512], BF16, base + 120832)
    PB = base + 153600
    wgb = [at(f"wgb{i}", [128, 32 * 128], BF16, PB + i * 8192) for i in range(2)]
    wub = [at(f"wub{i}", [128, 32 * 128], BF16, PB + 16384 + i * 8192) for i in range(2)]
    gsb = [at(f"gsb{i}", [128, 512], BF16, PB + 32768 + i * 1024) for i in range(2)]
    wdb = [at(f"wdb{i}", [128, 8 * 512], BF16, PB + i * 8192) for i in range(2)]
    wob = [at(f"wob{i}", [128, 4 * 512], BF16, PB + 16384 + i * 4096) for i in range(2)]
    xrb = [at(f"xrb{i}", [128, 512], F32, PB + 24576 + i * 2048) for i in range(2)]
    orb = [at(f"orb{i}", [128, 512], F32, PB + 28672 + i * 2048) for i in range(2)]
    assert PB + 34816 <= top

    pb_y = [e(nc.psum_tensor(f"pb_y{i}", [128, 512], F32)) for i in range(4)]
    pb_s = [e(nc.psum_tensor(f"pb_s{i}", [128, 512], F32)) for i in range(2)]
    pb_t = [e(nc.psum_tensor(f"pb_t{i}", [128, 256], BF16)) for i in range(2)]

    DMA("sync", lambda s: s.dma_start(out=sb_ident[:, :], in_=ident_in), [], s_ms)
    DMA("sync", lambda s: s.dma_start(out=sb_bq[:, :], in_=bq_in), [], s_ms)
    DMA("sync", lambda s: s.dma_start(out=sb_bk[:, :], in_=bk_in), [], s_ms)
    s_const = s_ms.count
    EM("gpsimd", lambda g: g.memset(sb_ones[:, :], 1.0), [], gA)
    EM("gpsimd", lambda g: g.memset(sb_bvp[:, :], 0.0), [], gA)
    EM("gpsimd", lambda g: g.memset(sb_bop[:, :], 0.0), [], gA)
    EM("gpsimd", lambda g: g.memset(sb_bc[:, :], 0.0), [], gA)
    g_const = gA.count
    DMA("gpsimd", lambda g: g.dma_start(out=sb_bvp[0:1, :], in_=bv_in), [(gA, g_const)], s_w)
    DMA("gpsimd", lambda g: g.dma_start(out=sb_bop[0:1, :], in_=bo_in), [(gA, g_const)], s_w)
    DMA("sync", lambda s: s.dma_start(out=kv_src.ap()[16384:16400, :],
        in_=sb_bop[16:32, 0:512]), [(gA, g_const)], s_pz)
    EM("vector", lambda v: v.tensor_scalar_mul(sb_bq[:, :], sb_bq[:, :], SCALE),
       [(s_ms, s_const)], vA)
    v_bq = vA.count

    a_lnapply = 0
    t_tr_done = {0: 0, 1: 0}
    v_tr_done = {0: 0, 1: 0}
    for tt in range(8):
        bp = tt % 2
        w = [(aA, a_lnapply)] if tt >= 2 else []
        DMA("sync", lambda s, tt=tt, bp=bp: s.dma_start(
            out=xt[bp][:, :], in_=x_in[tt * 128:(tt + 1) * 128, :]), w, s_x2[bp])
        wx = [(s_x2[bp], s_x2[bp].count)]
        EM("vector", lambda v, bp=bp: v.reduce_sum(smc(C_S1), xt[bp][:, :], axis=AX.X), wx, vA)
        EM("scalar", lambda s, bp=bp: s.activation(xsq[:, :], xt[bp][:, :], AF.Square),
           wx + [(vA, vA.count)], aA)
        EM("vector", lambda v: v.reduce_sum(smc(C_S2), xsq[:, :], axis=AX.X), [(aA, aA.count)], vA)
        EM("vector", lambda v: v.tensor_scalar_mul(smc(C_MEAN), smc(C_S1), 1.0 / D), [], vA)
        EM("vector", lambda v: v.tensor_mul(smc(C_M2), smc(C_MEAN), smc(C_MEAN)), [], vA)
        EM("vector", lambda v: v.tensor_scalar(smc(C_VAR), smc(C_S2), 1.0 / D, None, ALU.mult),
           [(aA, a_lnapply)], vA)
        EM("vector", lambda v: v.tensor_sub(smc(C_VAR), smc(C_VAR), smc(C_M2)), [], vA)
        EM("vector", lambda v: v.tensor_scalar_add(smc(C_VAR), smc(C_VAR), 1e-5), [], vA)
        EM("scalar", lambda s: s.activation(smc(C_STD), smc(C_VAR), AF.Sqrt), [(vA, vA.count)], aA)
        EM("vector", lambda v: v.reciprocal(smc(C_RSTD), smc(C_STD)),
           [(aA, aA.count)], vA)
        EM("vector", lambda v: v.tensor_mul(smc(C_MRS), smc(C_MEAN), smc(C_RSTD)), [], vA)
        EM("vector", lambda v: v.tensor_scalar_mul(smc(C_MRS), smc(C_MRS), -1.0), [], vA)
        EM("scalar", lambda s, bp=bp: s.activation(
            xnt[:, :], xt[bp][:, :], AF.Identity, bias=smc(C_MRS), scale=smc(C_RSTD)),
            [(vA, vA.count), (tA, tA.count)], aA)
        a_lnapply = aA.count
        for c in range(32):
            pp = c % 2
            EM("tensor", lambda t, c=c, pp=pp: t.transpose(
                pb_t[pp][:, 0:128], xnt[:, c * 128:(c + 1) * 128], sb_ident[:, :]),
                ([(aA, a_lnapply), (s_ms, s_const)] if c == 0 else [])
                + [(vA, v_tr_done[pp])], tA)
            t_tr_done[pp] = tA.count
            EM("vector", lambda v, c=c, pp=pp, tt=tt: v.tensor_copy(
                xn_d[:, c * T + tt * 128: c * T + tt * 128 + 128], pb_t[pp][:, 0:128]),
                [(tA, t_tr_done[pp])], vA)
            v_tr_done[pp] = vA.count
    v_xn = vA.count

    if DEBUG_OUTS:
        DMA("sync", lambda s: s.dma_start(out=dbg_xn, in_=xn_d[:, :]), [(vA, v_xn)], s_ms)

    t_kproj = {}
    a_kevac = {0: 0, 1: 0}
    a_ksq_head = {}
    v_knorm = 0
    s_kvw_head = {}
    for h in range(H):
        bp = h % 2
        w = [(tA, t_kproj[h - 2])] if h >= 2 else [(vA, v_xn), (aA, a_lnapply)]
        DMA("gpsimd", lambda g, h=h, bp=bp: g.dma_start(
            out=wkb[bp][:, :], in_=wk_t[h].rearrange("a b c -> b a c")), w, s_wk[bp])
        w_w = (s_wk[bp], s_wk[bp].count)
        for th in range(2):
            for dc in range(32):
                waits = []
                if dc == 0:
                    waits = [w_w, (vA, v_xn)]
                    if h >= 1:
                        waits.append((aA, a_kevac[th]))
                EM("tensor", lambda t, th=th, dc=dc, bp=bp: t.matmul(
                    pb_y[th][:, :], wkb[bp][:, dc * 128:(dc + 1) * 128],
                    xn_d[:, dc * T + th * 512:dc * T + (th + 1) * 512],
                    start=(dc == 0), stop=(dc == 31)), waits, tA)
            waits = [(tA, tA.count)]
            if h >= 2 and th == 0:
                waits += [(s_kvw, s_kvw.count), (aA, a_ksq_head[h - 2])]
            EM("scalar", lambda s, th=th, h=h, bp=bp: s.activation(
                kqb[bp][:, th * 512:(th + 1) * 512], pb_y[th][:, :], AF.Identity,
                bias=sb_bk[:, h:h + 1]), waits, aA)
            a_kevac[th] = aA.count
        t_kproj[h] = tA.count
        a_kq = aA.count
        EM("scalar", lambda s, bp=bp: s.activation(ksq[:, :], kqb[bp][:, :], AF.Square),
           [(vA, v_knorm)], aA)
        a_ksq_head[h] = aA.count
        a_sq = aA.count
        for th in range(2):
            EM("tensor", lambda t, th=th: t.matmul(
                pb_s[th][0:1, 0:512], sb_ones[:, 0:1], ksq[:, th * 512:(th + 1) * 512],
                start=True, stop=True),
                ([(aA, a_sq)] + ([(vA, v_knorm)] if h >= 1 else [])) if th == 0 else [], tA)
            EM("vector", lambda v, th=th: v.reduce_max(
                (smc(C_T0) if th == 0 else smc(C_T1))[0:1, :], pb_s[th][0:1, 0:512], axis=AX.X),
                [(tA, tA.count)], vA)
        EM("vector", lambda v: v.tensor_max(smc(C_U1)[0:1, :], smc(C_T0)[0:1, :], smc(C_T1)[0:1, :]), [], vA)
        EM("vector", lambda v, h=h: v.tensor_copy(nacc[0:1, h:h + 1], smc(C_U1)[0:1, :]), [], vA)
        v_knorm = vA.count
        DMA("sync", lambda s, h=h, bp=bp: s.dma_start(out=kview[h], in_=kqb[bp][:, :]),
            [(aA, a_kq)], s_kvw)
        s_kvw_head[h] = s_kvw.count
    DMA("sync", lambda s: s.dma_start(out=norms_row, in_=nacc[0:1, 0:H]),
        [(vA, v_knorm), (s_pz, 16)], s_kvw)

    t_vmm = {0: 0, 1: 0}
    v_vevac = {0: 0, 1: 0}
    s_vdma = {0: 0, 1: 0}
    for hg in range(8):
        bw = hg % 2
        w = [(tA, t_vmm[bw])] if hg >= 2 else [(vA, v_xn)]
        DMA("gpsimd", lambda g, hg=hg, bw=bw: g.dma_start(
            out=wvb[bw][:, :], in_=wv_t[hg].rearrange("a b c -> b a c")), w, s_wv[bw])
        w_w = (s_wv[bw], s_wv[bw].count)
        for tt in range(8):
            pbv = pb_s[tt % 2]
            for dc in range(32):
                waits = []
                if dc == 0:
                    waits = [w_w, (vA, v_vevac[tt % 2])]
                    if hg == 0 and tt < 2:
                        waits.append((vA, v_knorm))
                EM("tensor", lambda t, tt=tt, dc=dc, bw=bw, pbv=pbv: t.matmul(
                    pbv[:, :], xn_d[:, dc * T + tt * 128:dc * T + tt * 128 + 128],
                    wvb[bw][:, dc * 512:(dc + 1) * 512],
                    start=(dc == 0), stop=False), waits, tA)
            EM("tensor", lambda t, hg=hg, pbv=pbv: t.matmul(
                pbv[:, :], sb_ones[:, :], sb_bvp[:, hg * 512:(hg + 1) * 512],
                start=False, stop=True),
                [(s_w, 32)] if (hg == 0 and tt == 0) else [], tA)
            t_vmm[bw] = tA.count
            EM("vector", lambda v, tt=tt, pbv=pbv: v.tensor_copy(vtb[tt % 2][:, :], pbv[:, :]),
               [(tA, tA.count), (s_kvw, s_kvw.count)], vA)
            v_vevac[tt % 2] = vA.count
            DMA("sync", lambda s, hg=hg, tt=tt: s.dma_start(
                out=vview[tt * 128:(tt + 1) * 128, hg * 512:(hg + 1) * 512],
                in_=vtb[tt % 2][:, :]), [(vA, vA.count)], s_kvw)
            s_vdma[tt % 2] = s_kvw.count
    t_p1, v_p1, a_p1 = tA.count, vA.count, aA.count

    EM("gpsimd", lambda g: g.collective_compute(
        "AllGather", ALU.bypass, replica_groups=[[0, 1, 2, 3], [4, 5, 6, 7]],
        ins=[kv_src.ap()], outs=[kv_gath.ap()]), [(s_kvw, s_kvw.count)], s_cc, 1)

    if DEBUG_OUTS:
        DMA("sync", lambda s: s.dma_start(out=dbg_kv, in_=kv_gath.ap()), [(s_cc, 1)], s_ms)

    p1_bar = [(tA, t_p1), (vA, v_p1), (aA, a_p1), (s_kvw, s_kvw.count), (s_w, s_w.count),
              (s_wk[0], s_wk[0].count), (s_wk[1], s_wk[1].count),
              (s_wv[0], s_wv[0].count), (s_wv[1], s_wv[1].count)]
    DMA("sync", lambda s: s.dma_start(out=sb_mask[:, :], in_=masks_in), p1_bar, s_ms)
    s_mask_ld = s_ms.count
    for b in range(2):
        for rs in range(32):
            EM("vector", lambda v, b=b, rs=rs: v.memset(
                vh[b][:, rs * VST + 128:rs * VST + 129], 1.0),
                p1_bar if (b == 0 and rs == 0) else [], vA)
    v_vones = vA.count
    DMA("sync", lambda s: s.dma_start(out=sb_nk[0:1, :], in_=gN,
        allow_slow_non_contiguous=True), [(s_cc, 1)], s_nk2)

    t_head = {}
    v_head = {}
    a_exphead = {}
    s_ot_head = {}
    v_stevac = {0: 0, 1: 0}
    a_stexp = {0: 0, 1: 0}
    v_oaug = {0: 0, 1: 0, 2: 0, 3: 0}
    v_otcopy = {0: 0, 1: 0}
    t_osb = 0
    v_mhb_prev = 0

    for h in range(H):
        bp = h % 2
        w = [(s_cc, 1)]
        if h >= 2:
            w += [(tA, t_head[h - 2])]
        for r4 in range(4):
            DMA("sync", lambda s, h=h, bp=bp, r4=r4: s.dma_start(
                out=kh[bp][:, r4 * 1024:(r4 + 1) * 1024], in_=gK[h][:, r4]),
                w if r4 == 0 else [], s_kh[bp])
        for r4 in range(4):
            DMA("sync", lambda s, h=h, bp=bp, r4=r4: s.dma_start(
                out=vh[bp].ap().rearrange("p (s c) -> p s c", c=VST)[:, r4 * 8:(r4 + 1) * 8, 0:128],
                in_=gV[h][:, r4]),
                [(vA, v_vones)] if (h < 2 and r4 == 0) else [], s_vh[bp])
        w_kv = (s_kh[bp], s_kh[bp].count)
        w_vv = (s_vh[bp], s_vh[bp].count)
        wv2 = [(tA, t_head[h - 2])] if h >= 2 else list(p1_bar)
        DMA("gpsimd", lambda g, h=h, bp=bp: g.dma_start(
            out=wqb[bp][:, :], in_=wq_t[h].rearrange("a b c -> b a c")), wv2, s_wq[bp])
        w_wq = (s_wq[bp], s_wq[bp].count)
        if h < 4:
            CROWS = E_W // 8 // 512 // 4
            DMA("gpsimd", lambda g, h=h: g.dma_start(
                out=ag_w_in.ap()[h * CROWS:(h + 1) * CROWS, :],
                in_=wslice[h * CROWS:(h + 1) * CROWS, :]), [], s_cv)
        if h == 3:
            EM("gpsimd", lambda g: g.collective_compute(
                "AllGather", ALU.bypass, replica_groups=[list(range(8))],
                ins=[ag_w_in.ap()], outs=[ag_w_out.ap()]),
                [(s_cv, s_cv.count)], s_cc, 1)

        for th in range(2):
            for dc in range(32):
                waits = []
                if dc == 0:
                    waits = [w_wq, (vA, v_xn)]
                    waits.append((aA, a_stexp[th]) if h >= 1 else (vA, v_p1))
                    if h == 0:
                        waits.append((vA, v_bq))
                EM("tensor", lambda t, th=th, dc=dc, bp=bp: t.matmul(
                    pb_s[th][:, :], wqb[bp][:, dc * 128:(dc + 1) * 128],
                    xn_d[:, dc * T + th * 512:dc * T + (th + 1) * 512],
                    start=(dc == 0), stop=(dc == 31)), waits, tA)
            waits = [(tA, tA.count)]
            if h >= 1 and th == 0:
                waits.append((aA, a_exphead[h - 1]))
            EM("scalar", lambda s, th=th, h=h: s.activation(
                qh[:, th * 512:(th + 1) * 512], pb_s[th][:, :], AF.Identity,
                bias=sb_bq[:, h:h + 1], scale=SCALE), waits, aA)
        EM("scalar", lambda s: s.activation(qsq[:, :], qh[:, :], AF.Square), [], aA)
        a_qsq = aA.count
        for th in range(2):
            EM("tensor", lambda t, th=th: t.matmul(
                pb_s[th][0:1, 0:512], sb_ones[:, 0:1], qsq[:, th * 512:(th + 1) * 512],
                start=True, stop=True),
                [(aA, a_qsq)] if th == 0 else [], tA)
            EM("vector", lambda v, th=th: v.reduce_max(
                (smc(C_T0) if th == 0 else smc(C_T1))[0:1, :], pb_s[th][0:1, 0:512], axis=AX.X),
                [(tA, tA.count)], vA)
        EM("vector", lambda v: v.tensor_max(smc(C_MQ)[0:1, :], smc(C_T0)[0:1, :], smc(C_T1)[0:1, :]), [], vA)
        EM("vector", lambda v, h=h: v.reduce_max(
            smc(C_MK)[0:1, :], sb_nk[0:1, h * 4:(h + 1) * 4], axis=AX.X),
            [(s_nk2, 16)], vA)
        EM("vector", lambda v: v.tensor_mul(smc(C_T0)[0:1, :], smc(C_MQ)[0:1, :], smc(C_MK)[0:1, :]), [], vA)
        EM("scalar", lambda s: s.activation(smc(C_T1)[0:1, :], smc(C_T0)[0:1, :], AF.Ln), [(vA, vA.count)], aA)
        EM("scalar", lambda s: s.activation(smc(C_T0)[0:1, :], smc(C_T1)[0:1, :], AF.Exp, scale=0.5), [], aA)
        EM("vector", lambda v: v.tensor_scalar_mul(smc(C_U1)[0:1, :], smc(C_T0)[0:1, :], -1.0),
           [(aA, aA.count)] + ([(aA, a_exphead[h - 1])] if h >= 1 else []), vA)
        EM("vector", lambda v: v.tensor_copy(sb_bc[0:1, 0:1], smc(C_U1)[0:1, :]), [], vA)
        EM("tensor", lambda t: t.matmul(pb_s[0][:, 0:1], sb_ones[:, :], sb_bc[:, 0:1],
                                        start=True, stop=True), [(vA, vA.count)], tA)
        EM("vector", lambda v: v.tensor_copy(smc(C_MHB), pb_s[0][:, 0:1]), [(tA, tA.count)], vA)
        v_mhb = vA.count
        v_mhb_prev = v_mhb

        first_fin = True
        a_exp_si = {}

        def emit_st(si):
            g, j, act, moff = SCHED[si]
            jp = si % 2
            rp = si % 4
            n = len(act)
            qlo, qhi = act[0] * 128, (act[-1] + 1) * 128
            r_, s_ = block_owner(j)
            pos = r_ * 8 + s_
            waits = [(aA, a_stexp[jp])]
            if si == 0:
                waits += [w_kv, w_vv, (s_ms, s_mask_ld), (vA, v_mhb)]
            EM("tensor", lambda t, jp=jp, pos=pos, qlo=qlo, qhi=qhi, bp=bp: t.matmul(
                pb_s[jp][:, 0:qhi - qlo], kh[bp][:, pos * 128:(pos + 1) * 128],
                qh[:, qlo:qhi], start=True, stop=False), waits, tA)
            EM("tensor", lambda t, jp=jp, n=n, moff=moff: t.matmul(
                pb_s[jp][:, 0:n * 128], sb_ident[:, :],
                sb_mask[:, moff:moff + n * 128], start=False, stop=True), [], tA)
            t_st = tA.count
            EM("scalar", lambda s, jp=jp, rp=rp, n=n: s.activation(
                ptb[rp][:, 0:n * 128], pb_s[jp][:, 0:n * 128], AF.Exp, bias=smc(C_MHB)),
                [(tA, t_st)], aA)
            a_stexp[jp] = aA.count
            a_exp_si[si] = aA.count
            return si

        def emit_av(si):
            nonlocal first_fin, t_osb
            g, j, act, moff = SCHED[si]
            jp = si % 2
            rp = si % 4
            r_, s_ = block_owner(j)
            pos = r_ * 8 + s_
            for idx, sl in enumerate(act):
                c_in = sl % 4
                first, last = (j == 0), (j == PADS[sl] - 1)
                avw = [(aA, a_exp_si[si])] if idx == 0 else []
                if first:
                    avw.append((vA, v_oaug[c_in]))
                EM("tensor", lambda t, c_in=c_in, idx=idx, rp=rp, pos=pos, bp=bp,
                   first=first, last=last: t.matmul(
                    pb_y[c_in][:, 0:129], ptb[rp][:, idx * 128:(idx + 1) * 128],
                    vh[bp][:, pos * VST:pos * VST + 129],
                    start=first, stop=last), avw, tA)
                if last:
                    tb = sl % 2
                    EM("vector", lambda v, c_in=c_in: v.reciprocal(
                        smc(C_INVL), pb_y[c_in][:, 128:129]), [(tA, tA.count)], vA)
                    EM("vector", lambda v, c_in=c_in: v.tensor_scalar(
                        osb[:, :], pb_y[c_in][:, 0:128], smc(C_INVL), None, ALU.mult),
                        [(tA, t_osb)], vA)
                    v_oaug[c_in] = vA.count
                    EM("tensor", lambda t, tb=tb: t.transpose(
                        pb_t[tb][:, 0:128], osb[:, :], sb_ident[:, :]),
                        [(vA, v_oaug[c_in]), (vA, v_otcopy[tb])], tA)
                    t_osb = tA.count
                    cw = [(tA, t_osb)]
                    if first_fin and h >= 2:
                        cw.append((s_ot, s_ot.count))
                    first_fin = False
                    EM("vector", lambda v, sl=sl, bp=bp, tb=tb: v.tensor_copy(
                        oth[bp][:, sl * 128:(sl + 1) * 128], pb_t[tb][:, 0:128]), cw, vA)
                    v_otcopy[tb] = vA.count

        pend2 = []
        for si in range(len(SCHED)):
            emit_st(si)
            if len(pend2) == 2:
                emit_av(pend2.pop(0))
            pend2.append(si)
        for si in pend2:
            emit_av(si)
        a_exphead[h] = aA.count
        DMA("sync", lambda s, h=h, bp=bp: s.dma_start(
            out=ot_dram.ap()[h * 128:(h + 1) * 128, :], in_=oth[bp][:, :]),
            [(vA, vA.count)], s_ot)
        s_ot_head[h] = s_ot.count
        t_head[h] = tA.count
        v_head[h] = vA.count

    DMA("sync", lambda s: s.dma_start(out=xn_spill.ap(), in_=xn_d[:, :]),
        [(tA, tA.count)], s_sp)
    s_spill = s_sp.count
    if DEBUG_OUTS:
        DMA("sync", lambda s: s.dma_start(out=dbg_ot, in_=ot_dram.ap()),
            [(s_ot, s_ot.count)], s_ms)

    t_p3, v_p3, a_p3 = tA.count, vA.count, aA.count
    s_gr_p3, s_ot_p3, s_w_p3 = s_gr.count, s_ot.count, s_w.count

    BAR = [(tA, t_p3), (vA, v_p3), (aA, a_p3), (s_sp, s_spill),
           (s_ot, s_ot_p3), (s_ms, s_ms.count), (s_cc, 2), (s_cv, s_cv.count),
           (s_kh[0], s_kh[0].count), (s_kh[1], s_kh[1].count),
           (s_vh[0], s_vh[0].count), (s_vh[1], s_vh[1].count),
           (s_wq[0], s_wq[0].count), (s_wq[1], s_wq[1].count), (s_nk2, 16), (s_w, 32)]

    t_half_end = 0
    t_gate = {0: 0, 1: 0}
    a_silu = 0
    v_hmul = 0
    t_wd = {0: 0, 1: 0}
    t_wo = {0: 0, 1: 0}
    v_yevac = {0: 0, 1: 0, 2: 0, 3: 0}
    s_orb = {0: 0, 1: 0}
    v_orb = {0: 0, 1: 0}

    v_b_end = 0
    s_out_b = 0
    for th in range(2):
        hb = list(BAR) if th == 0 else [(tA, t_half_end), (vA, v_b_end), (s_out, s_out_b)]
        DMA("sync", lambda s, th=th: s.dma_start(
            out=xnh[:, :],
            in_=xn_spill.ap().rearrange("p (d t) -> p d t", d=32)[:, :, th * 512:(th + 1) * 512]),
            hb + [(s_sp, s_spill)], s_sp)
        w_xn = (s_sp, s_sp.count)
        DMA("sync", lambda s, th=th: s.dma_start(
            out=otx[:, :],
            in_=ot_dram.ap().rearrange("(ic p) t -> p ic t", p=128)[:, :, th * 512:(th + 1) * 512]),
            hb, s_ot)
        w_ot = (s_ot, s_ot.count)

        for fb in range(NFB):
            bpf = fb % 2
            if fb < 2:
                w = list(hb)
            else:
                w = [(tA, t_gate[bpf])]
            DMA("sync", lambda s, fb=fb, bpf=bpf: s.dma_start(
                out=wgb[bpf][:, :], in_=wg_view[fb]), w, s_wg2[bpf])
            DMA("scalar", lambda s, fb=fb, bpf=bpf: s.dma_start(
                out=wub[bpf][:, :], in_=wu_view[fb]), list(w), s_wu2[bpf])
            w_wf = (s_wg2[bpf], s_wg2[bpf].count)
            w_wf2 = (s_wu2[bpf], s_wu2[bpf].count)
            for dc in range(32):
                waits = [w_wf, w_wf2, w_xn, (aA, a_silu)] if dc == 0 else []
                EM("tensor", lambda t, dc=dc, bpf=bpf: t.matmul(
                    pb_s[0][:, :], wgb[bpf][:, dc * 128:(dc + 1) * 128],
                    xnh[:, dc * 512:(dc + 1) * 512],
                    start=(dc == 0), stop=(dc == 31)), waits, tA)
            t_g = tA.count
            for dc in range(32):
                waits = [(vA, v_hmul)] if dc == 0 else []
                EM("tensor", lambda t, dc=dc, bpf=bpf: t.matmul(
                    pb_s[1][:, :], wub[bpf][:, dc * 128:(dc + 1) * 128],
                    xnh[:, dc * 512:(dc + 1) * 512],
                    start=(dc == 0), stop=(dc == 31)), waits, tA)
            t_gate[bpf] = tA.count
            EM("scalar", lambda s, bpf=bpf: s.activation(
                gsb[bpf][:, :], pb_s[0][:, :], AF.Silu),
                [(tA, t_g), (vA, v_hmul)], aA)
            a_silu = aA.count
            EM("vector", lambda v, fb=fb, bpf=bpf: v.tensor_tensor(
                h_sb[:, fb * 512:(fb + 1) * 512], pb_s[1][:, :], gsb[bpf][:, :], ALU.mult),
                [(tA, t_gate[bpf]), (aA, a_silu)], vA)
            v_hmul = vA.count
        v_h_done = vA.count
        t_a_end = tA.count

        for do in range(8):
            w_wd = None
            for fb in range(NFB):
                kb = fb // 8
                bwd = kb % 2
                if fb % 8 == 0:
                    lo, hi = kb * 8, min(NFB, kb * 8 + 8)
                    dw = [(tA, t_wd[bwd])]
                    if do == 0 and kb == 0:
                        dw += [w_ot, (vA, v_h_done), (aA, a_silu), (tA, t_a_end)]
                    DMA("sync", lambda s, do=do, lo=lo, hi=hi, bwd=bwd: s.dma_start(
                        out=wdb[bwd][:, 0:(hi - lo) * 512],
                        in_=wd_view[do][:, lo:hi, :]), dw, s_wd2[bwd])
                    w_wd = (s_wd2[bwd], s_wd2[bwd].count)
                for tt in range(4):
                    waits = []
                    if fb % 8 == 0 and tt == 0:
                        waits = [w_wd, (vA, v_h_done)]
                        if fb == 0:
                            waits.append((vA, v_yevac[3]))
                    EM("tensor", lambda t, fb=fb, tt=tt, bwd=bwd: t.matmul(
                        pb_y[tt][:, :], h_sb[:, fb * 512 + tt * 128:fb * 512 + tt * 128 + 128],
                        wdb[bwd][:, (fb % 8) * 512:(fb % 8 + 1) * 512],
                        start=(fb == 0), stop=False), waits, tA)
                if fb % 8 == 7 or fb == NFB - 1:
                    t_wd[bwd] = tA.count
            for ic in range(32):
                kb = ic // 4
                bwo = kb % 2
                if ic % 4 == 0:
                    dw = [(tA, t_wo[bwo])]
                    if do == 0 and kb == 0:
                        dw += [(aA, a_silu), (tA, t_a_end)]
                    DMA("scalar", lambda s, do=do, kb=kb, bwo=bwo: s.dma_start(
                        out=wob[bwo][:, :], in_=wo_view[do][:, kb * 4:(kb + 1) * 4, :]), dw, s_wo2[bwo])
                    w_wo = (s_wo2[bwo], s_wo2[bwo].count)
                for tt in range(4):
                    waits = [w_wo] if (ic % 4 == 0 and tt == 0) else []
                    EM("tensor", lambda t, ic=ic, tt=tt, bwo=bwo: t.matmul(
                        pb_y[tt][:, :], otx[:, ic * 512 + tt * 128:ic * 512 + tt * 128 + 128],
                        wob[bwo][:, (ic % 4) * 512:(ic % 4 + 1) * 512],
                        start=False, stop=False), waits, tA)
                if ic % 4 == 3:
                    t_wo[bwo] = tA.count
            for tt in range(4):
                EM("tensor", lambda t, do=do, tt=tt: t.matmul(
                    pb_y[tt][:, :], sb_ones[:, :], sb_bop[:, do * 512:(do + 1) * 512],
                    start=False, stop=True), [], tA)
            t_bo = tA.count
            for tt in range(4):
                ob = tt % 2
                DMA("scalar", lambda s, th=th, do=do, tt=tt, ob=ob: s.dma_start(
                    out=xrb[ob][:, :],
                    in_=x_in[th * 512 + tt * 128:th * 512 + tt * 128 + 128,
                             do * 512:(do + 1) * 512]),
                    [(vA, v_orb[ob])] + ([(tA, t_a_end)] if (do == 0 and tt < 2) else []),
                    s_xr2[ob])
                EM("vector", lambda v, tt=tt, ob=ob: v.tensor_tensor(
                    orb[ob][:, :], pb_y[tt][:, :], xrb[ob][:, :], ALU.add),
                    [(tA, t_bo), (s_xr2[ob], s_xr2[ob].count), (s_out, s_out.count)], vA)
                v_yevac[tt] = vA.count
                v_orb[ob] = vA.count
                DMA("scalar", lambda s, th=th, do=do, tt=tt, ob=ob: s.dma_start(
                    out=out_dram[th * 512 + tt * 128:th * 512 + tt * 128 + 128,
                                 do * 512:(do + 1) * 512],
                    in_=orb[ob][:, :]), [(vA, vA.count)], s_out)
                s_orb[ob] = s_out.count
        t_half_end = tA.count
        v_b_end = vA.count
        s_out_b = s_out.count

    EM("scalar", None, [(s_out, s_out.count)])
    EM("sync", None, [(s_out, s_out.count)])
    EM("vector", None, [(s_out, s_out.count)])

    targets = rec.collect_targets()
    with nc.Block() as block:
        block.tensor(lambda engine: rec.replay("tensor", engine, targets))
        block.vector(lambda engine: rec.replay("vector", engine, targets))
        block.scalar(lambda engine: rec.replay("scalar", engine, targets))
        block.gpsimd(lambda engine: rec.replay("gpsimd", engine, targets))
        block.sync(lambda engine: rec.replay("sync", engine, targets))

    return nc, st



def _tile_w(w, cols):
    o, i = w.shape
    return np.ascontiguousarray(
        w.reshape(o // cols, cols, i // 128, 128).transpose(0, 2, 3, 1))


def build_masks(r):
    blks = rank_blocks(r)
    m = np.zeros((128, MASK_COLS), dtype=np.float32)
    kk = np.arange(128)[:, None]
    qq = np.arange(128)[None, :]
    tri = np.where(kk <= qq, 0.0, NEG).astype(np.float32)
    for (g, j, act, off) in SCHED:
        for idx, sl in enumerate(act):
            gq = blks[sl]
            col = off + idx * 128
            if j == gq:
                m[:, col:col + 128] = tri
            elif j > gq:
                m[:, col:col + 128] = NEG
    return m.astype(ml_dtypes.bfloat16)


_PROG_CACHE = {}


def _get_prog():
    if "p" not in _PROG_CACHE:
        _PROG_CACHE["p"] = build_program()
    return _PROG_CACHE["p"][0]


def kernel(L0_attn_x, L0_attn_Wq, L0_attn_bq, L0_attn_Wk, L0_attn_bk,
           L0_attn_Wv, L0_attn_bv, L0_attn_Wo, L0_attn_bo,
           L0_ffn_W_gate, L0_ffn_W_up, L0_ffn_W_down):
    x = np.asarray(L0_attn_x, np.float32)
    wq_t = _tile_w(np.asarray(L0_attn_Wq, np.float32), 128)
    wk_t = _tile_w(np.asarray(L0_attn_Wk, np.float32), 128)
    wv_t = _tile_w(np.asarray(L0_attn_Wv, np.float32), 512)
    wo_t = _tile_w(np.asarray(L0_attn_Wo, np.float32), 512)
    wg_t = _tile_w(np.asarray(L0_ffn_W_gate, np.float32), 128)
    wu_t = _tile_w(np.asarray(L0_ffn_W_up, np.float32), 128)
    wd_t = _tile_w(np.asarray(L0_ffn_W_down, np.float32), 512)
    wflat = np.concatenate([wg_t.ravel(), wu_t.ravel(), wd_t.ravel(), wo_t.ravel()])
    wflat = wflat.reshape(E_W // 512, 512)
    WR = E_W // 8 // 512
    bq_t = np.ascontiguousarray(np.asarray(L0_attn_bq, np.float32).reshape(32, 128).T)
    bk_t = np.ascontiguousarray(np.asarray(L0_attn_bk, np.float32).reshape(32, 128).T)
    bv_row = np.ascontiguousarray(np.asarray(L0_attn_bv, np.float32).reshape(1, D))
    bo_row = np.ascontiguousarray(np.asarray(L0_attn_bo, np.float32).reshape(1, D))
    ident = np.eye(128, dtype=ml_dtypes.bfloat16)

    nc = _get_prog()
    in_maps = []
    for c in range(8):
        b, r = c // 4, c % 4
        blks = rank_blocks(r)
        rows = np.concatenate([np.arange(j * 128, (j + 1) * 128) for j in blks])
        in_maps.append(dict(
            x_core=np.ascontiguousarray(x[b][rows]),
            wq_t=wq_t, wk_t=wk_t, wv_t=wv_t,
            wslice=np.ascontiguousarray(wflat[c * WR:(c + 1) * WR]),
            bq_t=bq_t, bk_t=bk_t, bv_row=bv_row, bo_row=bo_row,
            masks_t=build_masks(r), ident=ident,
        ))
    res = run_bass_kernel_spmd(nc, in_maps, core_ids=list(range(8)))
    out = np.empty((B, L, D), dtype=np.float32)
    for c in range(8):
        b, r = c // 4, c % 4
        blks = rank_blocks(r)
        oc = res.results[c]["out_core"]
        for sl, j in enumerate(blks):
            out[b, j * 128:(j + 1) * 128, :] = oc[sl * 128:(sl + 1) * 128, :]
    if DEBUG_OUTS:
        kernel.last_results = res
    return out
